# version 02
# baseline (speedup 1.0000x reference)
import os
import numpy as np
import ml_dtypes
from contextlib import ExitStack
import concourse.bass as bass
import concourse.mybir as mybir
from concourse.bass_utils import run_bass_kernel_spmd

F32 = mybir.dt.float32
BF16 = mybir.dt.bfloat16
F16 = mybir.dt.float16
U8 = mybir.dt.uint8
AF = mybir.ActivationFunctionType
ALU = mybir.AluOpType
AXX = mybir.AxisListType.X

NCORE = 8
D = 2048
R = 1024
SC = 8
H = 16
DH = 128
L = 512
I_FF = 5632
OSH = 1408
OCH = 11
DSH = 256
ICH = 44
DMC = 16
INV_SQRT_DH = float(2.0 ** -3.5)
EXP_BIAS = -12.0

KPART_E = H * DH * R
VPART_E = R * D
KVSEC_E = KPART_E + VPART_E
GUPART_E = D * OSH
DNPART_E = DSH * I_FF
WSEC_E = GUPART_E + DNPART_E

ENGS = ("sync", "gpsimd", "scalar", "vector", "tensor")
UPTO = "full"


class Plan:
    def __init__(self, tot):
        self.stream = {e: [] for e in ENGS}
        self.tot = tot

    def add(self, eng, fn, waits=None, inc=None, incn=1):
        self.stream[eng].append(
            (fn, [w for w in (waits or []) if w is not None], inc, incn))
        if inc is not None:
            self.tot[inc] = self.tot.get(inc, 0) + incn
            return self.tot[inc]
        return None

    def dma(self, eng, out, in_, waits=None, **kw):
        sem = "qG" if eng == "gpsimd" else "qS"
        return self.add(eng, lambda e: e.dma_start(out=out, in_=in_, **kw),
                        waits, sem, 16)

    def emit(self, nc, sems):
        with nc.Block() as blk:
            for eng_name in ENGS:
                ops = self.stream[eng_name]
                if not ops:
                    continue

                def make_section(ops):
                    def section(e):
                        seen = {}
                        for fn, waits, inc, incn in ops:
                            for sname, val in waits:
                                if val is None or val <= 0:
                                    continue
                                if seen.get(sname, -1) >= val:
                                    continue
                                e.wait_ge(sems[sname], val)
                                seen[sname] = val
                            ins = fn(e)
                            if inc is not None:
                                ins.then_inc(sems[inc], incn)
                    return section

                getattr(blk, eng_name)(make_section(ops))


def build_nc(upto="full"):
    ORDER = {"A0": 0, "A1": 0, "A": 0, "ADQ": 0, "B": 1, "CKV": 1, "CKT": 1,
             "CCT": 1, "CKP": 1, "C": 2, "C2": 3, "D1h": 4, "D2y": 4,
             "full": 4}
    lvl = ORDER[upto]
    skip_xnT = upto == "A0"
    skip_dq = upto in ("A0", "A1")
    nc = bass.Bass("TRN2", target_bir_lowering=False)

    x_in = nc.dram_tensor("x", [R, D], F32, kind="ExternalInput")
    wqT_in = nc.dram_tensor("wqT", [D, D], F32, kind="ExternalInput")
    wkvT_in = nc.dram_tensor("wkvT", [D, L], F32, kind="ExternalInput")
    wupkT_in = nc.dram_tensor("wupkT", [L, D], F32, kind="ExternalInput")
    wupvT_in = nc.dram_tensor("wupvT", [L, D], F32, kind="ExternalInput")
    woT_in = nc.dram_tensor("woT", [D, D], F32, kind="ExternalInput")
    gu_u8_in = nc.dram_tensor("gu_u8", [OSH, D // 2], U8, kind="ExternalInput")
    gu_sc_in = nc.dram_tensor("gu_sc", [OSH, DMC], F16, kind="ExternalInput")
    dn_u8_in = nc.dram_tensor("dn_u8", [DSH, I_FF // 2], U8, kind="ExternalInput")
    dn_sc_in = nc.dram_tensor("dn_sc", [DSH, ICH], F16, kind="ExternalInput")
    y_out = nc.dram_tensor("y", [R, D], F32, kind="ExternalOutput")

    ident_h = nc.inline_tensor(np.eye(128, dtype=ml_dtypes.bfloat16), "ident128")

    xn_hbm = nc.dram_tensor("xn_hbm", [R, D], BF16, kind="Internal")
    xn2_hbm = nc.dram_tensor("xn2_hbm", [R, D], BF16, kind="Internal")
    x2_hbm = nc.dram_tensor("x2_hbm", [R, D], F32, kind="Internal")
    yT_hbm = nc.dram_tensor("yT_hbm", [D, R], BF16, kind="Internal")
    kv_part = nc.dram_tensor("kv_part", [KVSEC_E], BF16, kind="Internal")
    kv_full = nc.dram_tensor("kv_full", [NCORE * KVSEC_E], BF16,
                             kind="Internal")
    w_part = nc.dram_tensor("w_part", [WSEC_E], BF16, kind="Internal")
    w_full = nc.dram_tensor("w_full", [NCORE * WSEC_E], BF16,
                            kind="Internal")

    tot = {}
    ck = {}

    def mark(name, sem):
        ck[name] = (sem, tot.get(sem, 0))

    def at(name):
        return ck[name]

    with ExitStack() as _st0:
        en = _st0.enter_context
        _qS = en(nc.semaphore("qS"))
        _qG = en(nc.semaphore("qG"))
        _cV = en(nc.semaphore("cV"))
        _cS = en(nc.semaphore("cS"))
        _cT = en(nc.semaphore("cT"))
        _cAG = en(nc.semaphore("cAG"))
        xnT = en(nc.sbuf_tensor("xnT", [128, DMC * R], BF16))
        ident = en(nc.sbuf_tensor("ident", [128, 128], BF16))
        stats = en(nc.sbuf_tensor("stats", [128, 80], F32))
        junk = en(nc.sbuf_tensor("junk", [128, D], F32))
        lnout = en(nc.sbuf_tensor("lnout", [128, D], F32))
        consts = en(nc.sbuf_tensor("consts", [128, 4], F32))
        CONSTVALS = (1e-5, -4.0, -6.0, EXP_BIAS)
        for _i, _v in enumerate(CONSTVALS):
            nc.const_aps.register(F32, _v, consts[:, _i:_i + 1])
        
        sems = {"qS": _qS, "qG": _qG, "cV": _cV, "cS": _cS, "cT": _cT,
                "cAG": _cAG}

        def plan_ln(p, sc, src, dst, ready_waits, eps=1e-5, apply_waits=None):
            v_sq = p.add("scalar",
                         lambda e: e.activation(out=junk[:], in_=src,
                                                func=AF.Square,
                                                accum_out=stats[:, 8 + sc:9 + sc]),
                         waits=ready_waits, inc="cS")
            v_red = p.add("vector",
                          lambda e: e.reduce_sum(stats[:, 0 + sc:1 + sc], src,
                                                 axis=AXX),
                          waits=ready_waits, inc="cV")

            def small(e):
                mu = stats[:, 16 + sc:17 + sc]
                var = stats[:, 24 + sc:25 + sc]
                tmp = stats[:, 64:65]
                e.tensor_scalar_mul(mu, stats[:, 0 + sc:1 + sc], 1.0 / D)
                e.tensor_scalar_mul(var, stats[:, 8 + sc:9 + sc], 1.0 / D)
                e.tensor_tensor(tmp, mu, mu, ALU.mult)
                e.engine_nop()
                return e.tensor_tensor(var, var, tmp, ALU.subtract)
            v_var = p.add("vector", small, waits=[("cS", v_sq), ("cV", v_red)],
                          inc="cV")
            v_std = p.add("scalar",
                          lambda e: e.activation(out=stats[:, 48 + sc:49 + sc],
                                                 in_=stats[:, 24 + sc:25 + sc],
                                                 func=AF.Sqrt, bias=eps),
                          waits=[("cV", v_var)], inc="cS")

            def small2(e):
                rstd = stats[:, 32 + sc:33 + sc]
                nmr = stats[:, 40 + sc:41 + sc]
                e.reciprocal(rstd, stats[:, 48 + sc:49 + sc])
                e.engine_nop()
                return e.scalar_tensor_tensor(nmr, stats[:, 16 + sc:17 + sc],
                                              -1.0, rstd, ALU.mult, ALU.mult)
            v_rs = p.add("vector", small2, waits=[("cS", v_std)], inc="cV")
            v_apv = p.add("vector",
                          lambda e: e.tensor_scalar(
                              dst, src, stats[:, 32 + sc:33 + sc],
                              stats[:, 40 + sc:41 + sc], ALU.mult, ALU.add),
                          waits=(apply_waits or []), inc="cV")
            return p.add("scalar", lambda e: e.nop(),
                         waits=[("cV", v_apv)], inc="cS")

        def dump_and_finish(src_ap_hbm_bf16=None, src_sbuf=None):
            p2 = Plan(tot)
            if src_ap_hbm_bf16 is not None:
                n = 1
                for s in src_ap_hbm_bf16.shape:
                    n *= s
                dst = y_out[:].flatten()[0:n]
                for s in reversed(src_ap_hbm_bf16.shape[1:]):
                    pass
                dst = dst.rearrange("(a b) -> a b", a=src_ap_hbm_bf16.shape[0])
                if len(src_ap_hbm_bf16.shape) == 3:
                    dst = dst.rearrange("a (b c) -> a b c",
                                        b=src_ap_hbm_bf16.shape[1])
                p2.dma("gpsimd", dst, src_ap_hbm_bf16, waits=[])
            else:
                Fd = src_sbuf.shape[1]
                n = 128 * Fd
                dst = y_out[:].flatten()[0:n].rearrange("(q f) -> q f", q=128)
                p2.dma("gpsimd", dst, src_sbuf, waits=[])
            p2.add("gpsimd", lambda e: e.nop(), waits=[("qG", tot["qG"])])
            p2.emit(nc, sems)

        with ExitStack() as _stA:
            en = _stA.enter_context
            xwin = en(nc.sbuf_tensor("xwin", [128, 2 * D], F32))
            xnwin = en(nc.sbuf_tensor("xnwin", [128, 2 * D], BF16))
            bwin = en(nc.sbuf_tensor("bwin", [128, 2 * 1408], U8))
            nlo_u8 = en(nc.sbuf_tensor("nlo", [128, 1408], U8))
            nhi_u8 = en(nc.sbuf_tensor("nhi", [128, 1408], U8))
            nf2 = [en(nc.sbuf_tensor(f"nf{pp}", [128, 1408], F32))
                   for pp in range(2)]
            ge2 = [en(nc.sbuf_tensor(f"ge{pp}", [128, 1408], F32))
                   for pp in range(2)]
            m2 = [en(nc.sbuf_tensor(f"m{pp}", [128, 1408], F32))
                  for pp in range(2)]
            sg2 = [en(nc.sbuf_tensor(f"sg{pp}", [128, 1408], F32))
                   for pp in range(2)]
            a2 = [en(nc.sbuf_tensor(f"a{pp}", [128, 1408], F32))
                  for pp in range(2)]
            b2 = [en(nc.sbuf_tensor(f"b{pp}", [128, 1408], F32))
                  for pp in range(2)]
            winter = en(nc.sbuf_tensor("winter", [128, I_FF], BF16))
            tstage = en(nc.sbuf_tensor("tstage", [128, ICH * 128], BF16))
            scw = en(nc.sbuf_tensor("scw", [128, ICH], F32))
            scw16 = en(nc.sbuf_tensor("scw16", [128, ICH], F16))
            pstr_t = [en(nc.psum_tensor(f"pstr{pp}", [128, 128], BF16))
                      for pp in range(2)]
            p = Plan(tot)
            v_consts = p.add("vector",
                             lambda e: [e.memset(consts[:, _i:_i + 1], _v)
                                        for _i, _v in enumerate(CONSTVALS)][-1],
                             inc="cV")
            mark("consts", "cV")
            ld_id = p.dma("sync", ident[:], ident_h[:])

            xload, xn_done = {}, {}
            for sc in range(SC):
                w = [("cS", xn_done[sc - 2])] if sc >= 2 else []
                xload[sc] = p.dma("sync", xwin[:, (sc % 2) * D:(sc % 2) * D + D],
                                  x_in[sc * 128:(sc + 1) * 128, :], waits=w)
                src = xwin[:, (sc % 2) * D:(sc % 2) * D + D]
                dst = xnwin[:, (sc % 2) * D:(sc % 2) * D + D]
                xn_done[sc] = plan_ln(p, sc, src, dst, [("qS", xload[sc])])
                p.dma("gpsimd", xn_hbm[sc * 128:(sc + 1) * 128, :], dst,
                      waits=[("cS", xn_done[sc])])
            mark("xn_written", "qG")

            if not skip_xnT:
                for dmc in range(DMC):
                    p.dma("sync", xnT[:, dmc * R:(dmc + 1) * R],
                          xn_hbm[:, dmc * 128:(dmc + 1) * 128],
                          waits=[at("xn_written")], transpose=True)
            mark("xnT_ready", "qS")

            dq_chunks = [("gu", i) for i in range(OCH)] + \
                        [("dn", i) for i in range(4)]
            if skip_dq:
                dq_chunks = []
            _dqn = os.environ.get("DQN")
            if _dqn is not None:
                dq_chunks = dq_chunks[:int(_dqn)]
            tr_done, nib_done, ev_done, sc_used = {}, {}, {}, {}
            tp = 0
            for ci, (kind, i) in enumerate(dq_chunks):
                F = 1024 if kind == "gu" else 1408
                ngrp = DMC if kind == "gu" else ICH
                w = [("cV", nib_done[ci - 2])] if ci >= 2 else []
                bw = bwin[:, (ci % 2) * 1408:(ci % 2) * 1408 + F]
                if kind == "gu":
                    bsrc = gu_u8_in[i * 128:(i + 1) * 128, :]
                    ssrc = gu_sc_in[i * 128:(i + 1) * 128, :]
                else:
                    dmci, half = divmod(i, 2)
                    bsrc = dn_u8_in[dmci * 128:(dmci + 1) * 128,
                                    half * 1408:(half + 1) * 1408]
                    ssrc = dn_sc_in[dmci * 128:(dmci + 1) * 128, :]
                p.dma("sync", bw, bsrc, waits=w)
                scf = [("cV", sc_used[ci - 1])] if ci >= 1 else []
                p.dma("sync", scw16[:, 0:ngrp], ssrc, waits=scf)
                ldv = tot["qS"]

                v_cs = p.add("vector",
                             lambda e, ngrp=ngrp: e.tensor_copy(
                                 scw[:, 0:ngrp], scw16[:, 0:ngrp]),
                             waits=[("qS", ldv)], inc="cV")
                sc_used[ci] = v_cs

                v_ext = p.add("vector",
                              lambda e, bw=bw, F=F: (
                                  e.tensor_scalar(nlo_u8[:, 0:F], bw, 15,
                                                  None, ALU.bitwise_and),
                                  e.tensor_scalar(nhi_u8[:, 0:F], bw, 4, None,
                                                  ALU.logical_shift_right)
                              )[-1], inc="cV")

                if kind == "gu":
                    woff, wlen = (ci % 2) * D, D
                    gbase, ga = 0, DMC
                    wi_wait = [("cT", tr_done[ci - 2])] if ci >= 2 else []
                else:
                    woff, wlen = (i % 2) * 2816, 2816
                    gbase, ga = (i % 2) * 22, 22
                    wi_wait = [("cT", tr_done[10 if i < 2 else 12])]

                def stage1(e, F=F):
                    e.tensor_copy(nf2[0][:, 0:F], nlo_u8[:, 0:F])
                    e.tensor_copy(nf2[1][:, 0:F], nhi_u8[:, 0:F])
                    e.tensor_scalar(ge2[0][:, 0:F], nf2[0][:, 0:F], 8.0, None,
                                    ALU.is_ge)
                    e.tensor_scalar(ge2[1][:, 0:F], nf2[1][:, 0:F], 8.0, None,
                                    ALU.is_ge)
                    e.scalar_tensor_tensor(m2[0][:, 0:F], ge2[0][:, 0:F], -8.0,
                                           nf2[0][:, 0:F], ALU.mult, ALU.add)
                    e.scalar_tensor_tensor(m2[1][:, 0:F], ge2[1][:, 0:F], -8.0,
                                           nf2[1][:, 0:F], ALU.mult, ALU.add)
                    e.tensor_scalar(sg2[0][:, 0:F], ge2[0][:, 0:F], -2.0, 1.0,
                                    ALU.mult, ALU.add)
                    return e.tensor_scalar(sg2[1][:, 0:F], ge2[1][:, 0:F],
                                           -2.0, 1.0, ALU.mult, ALU.add)
                vm = p.add("vector", stage1, waits=[("cV", v_ext)], inc="cV")
                vr = p.add("scalar",
                           lambda e, F=F: (
                               e.activation(out=a2[0][:, 0:F],
                                            in_=m2[0][:, 0:F],
                                            func=AF.Relu, bias=-4.0),
                               e.activation(out=a2[1][:, 0:F],
                                            in_=m2[1][:, 0:F],
                                            func=AF.Relu, bias=-4.0),
                               e.activation(out=b2[0][:, 0:F],
                                            in_=m2[0][:, 0:F],
                                            func=AF.Relu, bias=-6.0),
                               e.activation(out=b2[1][:, 0:F],
                                            in_=m2[1][:, 0:F],
                                            func=AF.Relu, bias=-6.0))[-1],
                           waits=[("cV", vm)], inc="cS")

                def stage2(e, F=F, gbase=gbase, ga=ga, woff=woff, wlen=wlen):
                    for pp in range(2):
                        e.tensor_tensor(ge2[pp][:, 0:F], m2[pp][:, 0:F],
                                        a2[pp][:, 0:F], ALU.add)
                    for pp in range(2):
                        e.scalar_tensor_tensor(nf2[pp][:, 0:F],
                                               ge2[pp][:, 0:F], 0.5,
                                               b2[pp][:, 0:F],
                                               ALU.mult, ALU.add)
                    for pp in range(2):
                        e.tensor_tensor(a2[pp][:, 0:F], nf2[pp][:, 0:F],
                                        sg2[pp][:, 0:F], ALU.mult)
                    last = None
                    for pp in range(2):
                        scb = scw[:, gbase:gbase + ga].unsqueeze(
                            2).broadcast_to([128, ga, 64])
                        sv3 = a2[pp][:, 0:F].rearrange("q (a b) -> q a b",
                                                       a=ga)
                        wout = winter[:, woff:woff + wlen].rearrange(
                            "q (a b two) -> q a b two", a=ga,
                            two=2)[:, :, :, pp]
                        last = e.tensor_tensor(wout, sv3, scb, ALU.mult)
                    return last
                vw2 = p.add("vector", stage2,
                            waits=[("cS", vr), ("cV", v_cs)] + wi_wait,
                            inc="cV")
                nib_done[ci] = vw2

                if kind == "dn" and i % 2 == 0:
                    continue
                nblk = DMC if kind == "gu" else ICH
                base = woff if kind == "gu" else 0
                for b in range(nblk):
                    tw = [("cV", vw2), ("qS", ld_id)]
                    if tp >= 2:
                        tw.append(("cV", ev_done[tp - 2]))
                    vt = p.add("tensor",
                               lambda e, b=b, tp=tp, base=base: e.transpose(
                                   pstr_t[tp % 2][:],
                                   winter[:, base + b * 128:base + b * 128 + 128],
                                   ident[:]),
                               waits=tw, inc="cT")
                    ev_done[tp] = p.add(
                        "vector",
                        lambda e, b=b, tp=tp: e.tensor_copy(
                            tstage[:, b * 128:(b + 1) * 128],
                            pstr_t[tp % 2][:]),
                        waits=[("cT", vt)], inc="cV")
                    tp += 1
                tr_done[ci] = tot["cT"]
                stage_done = tot["cV"]

                if kind == "gu":
                    off = i * (128 * DMC * 128)
                    dst = w_part[off:off + 128 * DMC * 128].rearrange(
                        "(q f) -> q f", q=128)
                    p.dma("gpsimd", dst, tstage[:, 0:DMC * 128],
                          waits=[("cV", stage_done)])
                else:
                    dmci = i // 2
                    off = GUPART_E + dmci * (ICH * 128 * 128)
                    dst = w_part[off:off + ICH * 128 * 128].rearrange(
                        "(ic q f) -> q ic f", ic=ICH, q=128)
                    src = tstage[:].rearrange("q (ic f) -> q ic f", ic=ICH)
                    p.dma("gpsimd", dst, src, waits=[("cV", stage_done)])
            mark("w_part_written", "qG")
            if ev_done:
                p.add("tensor", lambda e: e.nop(),
                      waits=[("cV", max(ev_done.values()))])
            if lvl == 0:
                p.add("gpsimd", lambda e: e.nop(),
                      waits=[("qS", tot.get("qS", 0)), ("qG", tot.get("qG", 0)),
                             ("cV", tot.get("cV", 0)), ("cS", tot.get("cS", 0)),
                             ("cT", tot.get("cT", 0))])
            p.emit(nc, sems)
            if lvl == 0:
                if upto == "ADQ":
                    dump_and_finish(src_ap_hbm_bf16=w_part[0:R * D].rearrange(
                        "(q f) -> q f", q=R))
                else:
                    dump_and_finish(src_ap_hbm_bf16=xn_hbm[:])
                return nc

        with nc.sbuf_tensor("attoT", [128, H * R], BF16) as attoT:
            with nc.sbuf_tensor("qT", [128, H * R], BF16) as qT:
                with ExitStack() as _stB:
                    en = _stB.enter_context
                    wbuf = en(nc.sbuf_tensor("wbuf", [128, 16 * 1024], BF16))
                    cTsb = en(nc.sbuf_tensor("cTsb", [128, 4 * R], BF16))
                    kvT = en(nc.sbuf_tensor("kvT", [128, H * R], BF16))
                    vstage = en(nc.sbuf_tensor("vstage", [128, 2 * D], BF16))
                    psB = en(nc.psum_tensor("psB", [128, 2 * 512], F32))
                    p = Plan(tot)
                    for dmc in range(DMC):
                        p.dma("gpsimd", wbuf[:, dmc * 512:(dmc + 1) * 512],
                              wkvT_in[dmc * 128:(dmc + 1) * 128, :])
                    mark("w_kv", "qG")
                    for lc in range(4):
                        p.dma("gpsimd",
                              wbuf[:, 8192 + lc * D:8192 + (lc + 1) * D],
                              wupkT_in[lc * 128:(lc + 1) * 128, :])
                    mark("w_upk", "qG")

                    groups = []
                    for lc in range(4):
                        for sb in range(2):
                            groups.append((
                                [wbuf[:, dmc * 512 + lc * 128:
                                      dmc * 512 + lc * 128 + 128]
                                 for dmc in range(DMC)],
                                [xnT[:, dmc * R + sb * 512:
                                     dmc * R + sb * 512 + 512]
                                 for dmc in range(DMC)],
                                cTsb[:, lc * R + sb * 512:
                                     lc * R + sb * 512 + 512],
                                [at("w_kv"), at("xnT_ready")], ("c", lc, sb)))
                    NCG = len(groups)
                    for h in range(H):
                        for sb in range(2):
                            groups.append((
                                [wbuf[:, 8192 + lc * D + h * 128:
                                      8192 + lc * D + h * 128 + 128]
                                 for lc in range(4)],
                                [cTsb[:, lc * R + sb * 512:
                                      lc * R + sb * 512 + 512]
                                 for lc in range(4)],
                                kvT[:, h * R + sb * 512:h * R + sb * 512 + 512],
                                [at("w_upk")], ("k", h, sb)))
                    for tc in range(SC):
                        for nq in range(4):
                            groups.append((
                                [cTsb[:, lc * R + tc * 128:
                                      lc * R + tc * 128 + 128]
                                 for lc in range(4)],
                                [wbuf[:, lc * D + nq * 512:
                                      lc * D + nq * 512 + 512]
                                 for lc in range(4)],
                                vstage[:, (tc % 2) * D + nq * 512:
                                       (tc % 2) * D + nq * 512 + 512],
                                ["w_upv"], ("v", tc, nq)))

                    ev, vwrite = {}, {}

                    def plan_group(gi, lhs, rhs, dst, extra, tag):
                        par = gi % 2
                        w = [at(x) if isinstance(x, str) else x for x in extra]
                        if gi >= 2:
                            w.append(("cS", ev[gi - 2]))
                        if tag == ("k", 0, 0):
                            w.append(("cS", ev[NCG - 1]))
                        if tag[0] == "v" and tag[1] >= 2 and tag[2] == 0:
                            w.append(("qS", vwrite[tag[1] - 2]))

                        def mm(e, lhs=lhs, rhs=rhs, par=par):
                            nk = len(lhs)
                            for k in range(nk):
                                ins = e.matmul(psB[:, par * 512:par * 512 + 512],
                                               lhs[k], rhs[k],
                                               start=(k == 0),
                                               stop=(k == nk - 1))
                            return ins
                        vt = p.add("tensor", mm, waits=w, inc="cT")
                        ev[gi] = p.add(
                            "scalar",
                            lambda e, dst=dst, par=par: e.copy(
                                out=dst, in_=psB[:, par * 512:par * 512 + 512]),
                            waits=[("cT", vt)], inc="cS")
                        if tag[0] == "v" and tag[2] == 3:
                            tc = tag[1]
                            dst_d = kv_part[KPART_E + tc * 128 * D:
                                            KPART_E + (tc + 1) * 128 * D]
                            dst_d = dst_d.rearrange("(hh t dv) -> t hh dv",
                                                    hh=H, t=128)
                            src_s = vstage[:, (tc % 2) * D:(tc % 2) * D + D]
                            src_s = src_s.rearrange("t (hh dv) -> t hh dv", hh=H)
                            vwrite[tc] = p.dma("sync", dst_d, src_s,
                                               waits=[("cS", ev[gi])])
                    for gi in range(NCG + 2 * H):
                        plan_group(gi, *groups[gi])
                    for lc in range(4):
                        p.dma("gpsimd",
                              wbuf[:, lc * D:(lc + 1) * D],
                              wupvT_in[lc * 128:(lc + 1) * 128, :],
                              waits=[("cS", ev[NCG - 1])] if lc == 0 else [])
                    mark("w_upv", "qG")
                    for gi in range(NCG + 2 * H, len(groups)):
                        plan_group(gi, *groups[gi])
                    kev = ev[NCG + 2 * H - 1]
                    p.dma("sync",
                          kv_part[0:KPART_E].rearrange("(hh d t) -> d hh t",
                                                       hh=H, d=128),
                          kvT[:].rearrange("d (hh t) -> d hh t", hh=H),
                          waits=[("cS", kev)])
                    mark("kwrite", "qS")
                    b_mm_done = ev[len(groups) - 1]
                    mark("b_mm_done", "cS")

                    qev = {}
                    for half in range(2):
                        hw = [("cS", b_mm_done)] if half == 0 else \
                             [("cS", qev[15])]
                        for dmc in range(DMC):
                            p.dma("gpsimd",
                                  wbuf[:, dmc * 1024:(dmc + 1) * 1024],
                                  wqT_in[dmc * 128:(dmc + 1) * 128,
                                         half * 1024:(half + 1) * 1024],
                                  waits=hw if dmc == 0 else [])
                        mark(f"wq{half}", "qG")
                        for dqc8 in range(8):
                            dq = half * 8 + dqc8
                            for sb in range(2):
                                gi2 = dq * 2 + sb
                                par = gi2 % 2
                                w = [at(f"wq{half}")]
                                if gi2 >= 2:
                                    w.append(("cS", qev[gi2 - 2]))
                                else:
                                    w.append(("cS", b_mm_done))

                                def mmq(e, dqc8=dqc8, sb=sb, par=par):
                                    for dmc in range(DMC):
                                        ins = e.matmul(
                                            psB[:, par * 512:par * 512 + 512],
                                            wbuf[:, dmc * 1024 + dqc8 * 128:
                                                 dmc * 1024 + dqc8 * 128 + 128],
                                            xnT[:, dmc * R + sb * 512:
                                                dmc * R + sb * 512 + 512],
                                            start=(dmc == 0),
                                            stop=(dmc == DMC - 1))
                                    return ins
                                vt = p.add("tensor", mmq, waits=w, inc="cT")
                                qev[gi2] = p.add(
                                    "scalar",
                                    lambda e, dq=dq, sb=sb, par=par: e.copy(
                                        out=qT[:, dq * R + sb * 512:
                                               dq * R + sb * 512 + 512],
                                        in_=psB[:, par * 512:par * 512 + 512]),
                                    waits=[("cT", vt)], inc="cS")
                    for h in range(H):
                        ck[f"qTh{h}"] = ("cS", qev[2 * h + 1])
                    mark("qT_done", "cS")
                    if lvl == 1:
                        p.add("gpsimd", lambda e: e.nop(),
                              waits=[("qS", tot.get("qS", 0)), ("qG", tot.get("qG", 0)),
                                     ("cV", tot.get("cV", 0)), ("cS", tot.get("cS", 0)),
                                     ("cT", tot.get("cT", 0)), ("cAG", tot.get("cAG", 0))])
                        p.emit(nc, sems)
                        if upto == "CKP":
                            dump_and_finish(src_ap_hbm_bf16=kv_part[
                                0:KPART_E].rearrange("(hh d t) -> d hh t",
                                                     hh=H, d=128))
                        elif upto == "CKT":
                            dump_and_finish(src_sbuf=kvT[:])
                        elif upto == "CCT":
                            dump_and_finish(src_sbuf=cTsb[:])
                        elif upto == "CKV":
                            dump_and_finish(src_ap_hbm_bf16=kv_full[
                                0:KPART_E].rearrange("(hh d t) -> d hh t",
                                                     hh=H, d=128))
                        else:
                            dump_and_finish(src_sbuf=qT[:])
                        return nc

                    agw = [at("kwrite")] + \
                          [("qS", vwrite[tc]) for tc in range(SC)]
                    p.add("gpsimd",
                          lambda e: e.collective_compute(
                              "AllGather", ALU.bypass,
                              replica_groups=[list(range(NCORE))],
                              ins=[kv_part[:]], outs=[kv_full[:]]),
                          waits=agw, inc="cAG")
                    mark("ag_kv", "cAG")
                    if os.environ.get("SKIP_AGW"):
                        p.add("gpsimd", lambda e: e.nop(),
                              waits=[at("w_part_written")], inc="cAG")
                    else:
                        p.add("gpsimd",
                              lambda e: e.collective_compute(
                                  "AllGather", ALU.bypass,
                                  replica_groups=[list(range(NCORE))],
                                  ins=[w_part[:]], outs=[w_full[:]]),
                              waits=[at("w_part_written")], inc="cAG")
                    mark("ag_w", "cAG")
                    p.emit(nc, sems)

                with ExitStack() as _stC:
                    en = _stC.enter_context
                    kbuf = en(nc.sbuf_tensor("kbuf", [128, 2 * R], BF16))
                    vbuf = en(nc.sbuf_tensor("vbuf", [128, 2 * R], BF16))
                    pT = en(nc.sbuf_tensor("pT", [128, 2 * R], BF16))
                    accbA = en(nc.sbuf_tensor("accbA", [128, R], BF16))
                    accbB = en(nc.sbuf_tensor("accbB", [128, R], BF16))
                    accfA = en(nc.sbuf_tensor("accfA", [128, R], F32))
                    accfB = en(nc.sbuf_tensor("accfB", [128, R], F32))
                    accs = en(nc.sbuf_tensor("accs", [128, R], BF16))
                    onecol = en(nc.sbuf_tensor("onecol", [128, 1], BF16))
                    onerow = en(nc.sbuf_tensor("onerow", [1, 128], BF16))
                    rDrow = en(nc.sbuf_tensor("rDrow", [1, R], F32))
                    rDrowb = en(nc.sbuf_tensor("rDrowb", [1, R], BF16))
                    bcsb = en(nc.sbuf_tensor("bcsb", [128, R], F32))
                    psS0 = en(nc.psum_tensor("psS0", [128, R], F32))
                    psS1 = en(nc.psum_tensor("psS1", [128, R], F32))
                    psO0 = en(nc.psum_tensor("psO0", [128, R], F32))
                    psO1 = en(nc.psum_tensor("psO1", [128, R], F32))
                    p = Plan(tot)
                    psS = (psS0, psS1)
                    psO = (psO0, psO1)

                    v_ones = p.add("vector",
                                   lambda e: (e.memset(onecol[:], 1.0),
                                              e.memset(onerow[:], 1.0))[-1],
                                   waits=[at("qT_done")], inc="cV")

                    kv_ld, pv_slot, pv_chunk = {}, {}, {}
                    exp_done, acc_done, evac_done, recip_done = {}, {}, {}, {}

                    HEADS = list(range(H))
                    if os.environ.get("DBG_REVHEADS"):
                        HEADS = HEADS[::-1]
                    for hi in range(H):
                        h = HEADS[hi]
                        for r in range(NCORE):
                            u = hi * NCORE + r
                            w = [at("ag_w")]
                            if u == 0:
                                w.append(at("qT_done"))
                            if u >= 2:
                                w.append(("cT", pv_slot[u - 2]))
                            koff = r * KVSEC_E + h * (DH * R)
                            p.dma("sync",
                                  kbuf[:, (u % 2) * R:(u % 2) * R + R],
                                  kv_full[koff:koff + DH * R].rearrange(
                                      "(d t) -> d t", d=128),
                                  waits=w)
                            voff = r * KVSEC_E + KPART_E
                            src = kv_full[voff:voff + R * D].rearrange(
                                "(tc hh t dv) -> tc hh t dv", tc=SC, hh=H,
                                t=128)
                            src = src[:, h:h + 1].squeeze(1).transpose([1, 0, 2])
                            kv_ld[u] = p.dma(
                                "sync",
                                vbuf[:, (u % 2) * R:(u % 2) * R + R].rearrange(
                                    "t (tc dv) -> t tc dv", tc=SC),
                                src, waits=[])
                            for j in range(NCORE):
                                g = u * NCORE + j
                                par = g % 2
                                first = (r == 0 and j == 0)
                                w = [("qS", kv_ld[u]), ("cV", v_ones),
                                     at(f"qTh{h}") if (r == 0 and j == 0)
                                     else None, at("consts")]
                                if hi == 0 and first:
                                    w.append(at("qT_done"))
                                if g >= 2:
                                    w.append(("cS", exp_done[g - 2]))
                                if first and hi >= 1:
                                    w.append(("cV", recip_done[hi - 1]))
                                if r == 0 and j == 1 and hi >= 1:
                                    w.append(("cV", evac_done[hi - 1]))

                                def mmS(e, u=u, j=j, h=h, par=par):
                                    kt = kbuf[:, (u % 2) * R + j * 128:
                                              (u % 2) * R + j * 128 + 128]
                                    for sb in range(2):
                                        ins = e.matmul(
                                            psS[par][:, sb * 512:sb * 512 + 512],
                                            kt,
                                            qT[:, h * R + sb * 512:
                                               h * R + sb * 512 + 512],
                                            start=True, stop=True)
                                    return ins
                                v_s = p.add("tensor", mmS, waits=w, inc="cT")

                                we = [("cT", v_s)]
                                if g >= 2:
                                    we.append(("cT", pv_chunk[g - 2]))
                                    we.append(("cV", acc_done[g - 2]))
                                exp_done[g] = p.add(
                                    "scalar",
                                    lambda e, par=par: e.activation(
                                        out=pT[:, par * R:par * R + R],
                                        in_=psS[par][:], func=AF.Exp,
                                        scale=INV_SQRT_DH, bias=EXP_BIAS),
                                    waits=we, inc="cS")

                                def accop(e, j=j, par=par, r=r):
                                    src = pT[:, par * R:par * R + R]
                                    acc = accbA if j % 2 == 0 else accbB
                                    if j <= 1:
                                        ret = e.tensor_copy(acc[:], src)
                                    else:
                                        ret = e.tensor_tensor(acc[:], acc[:],
                                                              src, ALU.add)
                                    if j == NCORE - 1:
                                        if r == 0:
                                            e.tensor_copy(accfA[:], accbA[:])
                                            ret = e.tensor_copy(accfB[:],
                                                                accbB[:])
                                        else:
                                            e.tensor_tensor(accfA[:], accfA[:],
                                                            accbA[:], ALU.add)
                                            ret = e.tensor_tensor(
                                                accfB[:], accfB[:], accbB[:],
                                                ALU.add)
                                    return ret
                                acc_done[g] = p.add(
                                    "vector", accop,
                                    waits=[("cS", exp_done[g])], inc="cV")

                                wp = [("cS", exp_done[g])]
                                if first and hi >= 2:
                                    wp.append(("cV", evac_done[hi - 2]))

                                def mmPV(e, u=u, j=j, par=par, parO=hi % 2,
                                         first=first,
                                         last=(r == NCORE - 1 and
                                               j == NCORE - 1)):
                                    vt = vbuf[:, (u % 2) * R + j * 128:
                                              (u % 2) * R + j * 128 + 128]
                                    for sb in range(2):
                                        ins = e.matmul(
                                            psO[parO][:, sb * 512:
                                                      sb * 512 + 512],
                                            vt,
                                            pT[:, par * R + sb * 512:
                                               par * R + sb * 512 + 512],
                                            start=first, stop=last)
                                    return ins
                                v_pv = p.add("tensor", mmPV, waits=wp, inc="cT")
                                pv_chunk[g] = v_pv
                                if j == NCORE - 1:
                                    pv_slot[u] = v_pv

                        if os.environ.get("DBG_PUREPV"):
                            evac_done[hi] = p.add(
                                "vector",
                                lambda e, h=h, hi=hi: e.tensor_copy(
                                    attoT[:, h * R:(h + 1) * R],
                                    psO[hi % 2][:]),
                                waits=[("cT", pv_slot[hi * NCORE + 7])],
                                inc="cV")
                            recip_done[hi] = tot["cV"]
                            continue
                        v_as = p.add(
                            "vector",
                            lambda e: e.tensor_tensor(accs[:], accfA[:],
                                                      accfB[:], ALU.add),
                            waits=[("cV", acc_done[(hi * NCORE + 7) * NCORE + 7])],
                            inc="cV")
                        v_dm = p.add(
                            "tensor",
                            lambda e: [e.matmul(
                                psS0[0:1, sb * 512:sb * 512 + 512], onecol[:],
                                accs[:, sb * 512:sb * 512 + 512],
                                start=True, stop=True) for sb in range(2)][-1],
                            waits=[("cV", v_as),
                                   ("cS", exp_done[(hi * NCORE + 7) * NCORE + 6])],
                            inc="cT")
                        v_rc = p.add("vector",
                                     lambda e: e.reciprocal(rDrow[:],
                                                            psS0[0:1, :]),
                                     waits=[("cT", v_dm)], inc="cV")
                        recip_done[hi] = v_rc
                        v_rb = p.add("vector",
                                     lambda e: e.tensor_copy(rDrowb[:],
                                                             rDrow[:]),
                                     waits=[("cV", v_rc)], inc="cV")
                        v_bc = p.add(
                            "tensor",
                            lambda e: [e.matmul(
                                psS1[:, sb * 512:sb * 512 + 512], onerow[:],
                                rDrowb[0:1, sb * 512:sb * 512 + 512],
                                start=True, stop=True) for sb in range(2)][-1],
                            waits=[("cV", v_rb)], inc="cT")
                        v_cpb = p.add("vector",
                                      lambda e: e.tensor_copy(bcsb[:],
                                                              psS1[:]),
                                      waits=[("cT", v_bc)], inc="cV")
                        if os.environ.get("DBG_DUMPD"):
                            evac_done[hi] = p.add(
                                "vector",
                                lambda e, h=h: e.tensor_copy(
                                    attoT[:, h * R:(h + 1) * R], bcsb[:]),
                                waits=[("cV", v_cpb),
                                       ("cT", pv_slot[hi * NCORE + 7])],
                                inc="cV")
                        elif os.environ.get("DBG_NODIV"):
                            evac_done[hi] = p.add(
                                "vector",
                                lambda e, h=h, hi=hi: e.tensor_copy(
                                    attoT[:, h * R:(h + 1) * R],
                                    psO[hi % 2][:]),
                                waits=[("cV", v_cpb),
                                       ("cT", pv_slot[hi * NCORE + 7])],
                                inc="cV")
                        else:
                            evac_done[hi] = p.add(
                                "vector",
                                lambda e, h=h, hi=hi: e.tensor_tensor(
                                    attoT[:, h * R:(h + 1) * R],
                                    psO[hi % 2][:], bcsb[:], ALU.mult),
                                waits=[("cV", v_cpb),
                                       ("cT", pv_slot[hi * NCORE + 7])],
                                inc="cV")
                    mark("atto_done", "cV")
                    if lvl == 2:
                        p.add("gpsimd", lambda e: e.nop(),
                              waits=[("qS", tot.get("qS", 0)), ("qG", tot.get("qG", 0)),
                                     ("cV", tot.get("cV", 0)), ("cS", tot.get("cS", 0)),
                                     ("cT", tot.get("cT", 0)), ("cAG", tot.get("cAG", 0))])
                        p.emit(nc, sems)
                        if os.environ.get("DBG_DUMPQ"):
                            dump_and_finish(src_sbuf=qT[:])
                        else:
                            dump_and_finish(src_sbuf=attoT[:])
                        return nc
                    p.emit(nc, sems)

            with ExitStack() as _stC2:
                en = _stC2.enter_context
                woT_sb = en(nc.sbuf_tensor("woT_sb", [128, DMC * D], BF16))
                aproj = en(nc.sbuf_tensor("aproj", [128, 2 * D], F32))
                xres = en(nc.sbuf_tensor("xres", [128, 2 * D], F32))
                xn2w = en(nc.sbuf_tensor("xn2w", [128, 2 * D], BF16))
                psW = en(nc.psum_tensor("psW", [128, 2 * D], F32))
                p = Plan(tot)
                for dmc in range(DMC):
                    p.dma("gpsimd", woT_sb[:, dmc * D:(dmc + 1) * D],
                          woT_in[dmc * 128:(dmc + 1) * 128, :],
                          waits=[at("ag_w"), at("atto_done")] if dmc == 0
                          else [])
                mark("wo_loaded", "qG")

                wo_ev, ln2_ap, xn2_ap, x2w, xn2w_dma, xadd = \
                    {}, {}, {}, {}, {}, {}
                for sc in range(SC):
                    par = sc % 2
                    xw = []
                    if sc >= 2:
                        xw.append(("cS", xn2_ap[sc - 2]))
                        xw.append(("qG", x2w[sc - 2]))
                    xld = p.dma("sync", xres[:, par * D:par * D + D],
                                x_in[sc * 128:(sc + 1) * 128, :], waits=xw)
                    w = [at("wo_loaded"), at("atto_done")]
                    if sc >= 2:
                        w.append(("cV", wo_ev[sc - 2]))

                    def mmW(e, sc=sc, par=par):
                        for hh in range(H):
                            for nq in range(4):
                                ins = e.matmul(
                                    psW[:, par * D + nq * 512:
                                        par * D + nq * 512 + 512],
                                    attoT[:, hh * R + sc * 128:
                                          hh * R + sc * 128 + 128],
                                    woT_sb[:, hh * D + nq * 512:
                                           hh * D + nq * 512 + 512],
                                    start=(hh == 0), stop=(hh == H - 1))
                        return ins
                    v_mm = p.add("tensor", mmW, waits=w, inc="cT")
                    evw = [("cT", v_mm)]
                    if sc >= 2:
                        evw.append(("cS", ln2_ap[sc - 2]))
                    wo_ev[sc] = p.add("vector",
                                      lambda e, par=par: e.tensor_copy(
                                          aproj[:, par * D:par * D + D],
                                          psW[:, par * D:par * D + D]),
                                      waits=evw, inc="cV")
                    apw = [("cV", xadd[sc - 1])] if sc >= 1 else []
                    ln2_ap[sc] = plan_ln(p, sc, aproj[:, par * D:par * D + D],
                                         lnout[:], [("cV", wo_ev[sc])],
                                         apply_waits=apw)
                    v_x2 = p.add("vector",
                                 lambda e, par=par: e.tensor_tensor(
                                     xres[:, par * D:par * D + D],
                                     xres[:, par * D:par * D + D],
                                     lnout[:], ALU.add),
                                 waits=[("cS", ln2_ap[sc]), ("qS", xld)],
                                 inc="cV")
                    xadd[sc] = v_x2
                    x2w[sc] = p.dma("gpsimd",
                                    x2_hbm[sc * 128:(sc + 1) * 128, :],
                                    xres[:, par * D:par * D + D],
                                    waits=[("cV", v_x2)])
                    xn2_ap[sc] = plan_ln(p, sc, xres[:, par * D:par * D + D],
                                         xn2w[:, par * D:par * D + D],
                                         [("cV", v_x2)])
                    xn2w_dma[sc] = p.dma("gpsimd",
                                         xn2_hbm[sc * 128:(sc + 1) * 128, :],
                                         xn2w[:, par * D:par * D + D],
                                         waits=[("cS", xn2_ap[sc])])
                mark("x2_written", "qG")
                mark("xn2_written", "qG")
                for dmc in range(DMC):
                    p.dma("sync", xnT[:, dmc * R:(dmc + 1) * R],
                          xn2_hbm[:, dmc * 128:(dmc + 1) * 128],
                          waits=[at("xn2_written")], transpose=True)
                mark("xn2T_ready", "qS")
                if lvl == 3:
                    p.add("gpsimd", lambda e: e.nop(),
                          waits=[("qS", tot.get("qS", 0)), ("qG", tot.get("qG", 0)),
                                 ("cV", tot.get("cV", 0)), ("cS", tot.get("cS", 0)),
                                 ("cT", tot.get("cT", 0)), ("cAG", tot.get("cAG", 0))])
                    p.emit(nc, sems)
                    dump_and_finish(src_ap_hbm_bf16=xn2_hbm[:])
                    return nc
                p.emit(nc, sems)

        with ExitStack() as _stD:
            en = _stD.enter_context
            hT = en(nc.sbuf_tensor("hT", [128, ICH * R], BF16))
            wstage = en(nc.sbuf_tensor("wstage", [128, 4 * DMC * 128], BF16))
            sgbuf = en(nc.sbuf_tensor("sgbuf", [128, R], BF16))
            dnstage = en(nc.sbuf_tensor("dnstage", [128, 2 * ICH * 128], BF16))
            ytT = en(nc.sbuf_tensor("ytT", [128, 2 * R], BF16))
            ywin = en(nc.sbuf_tensor("ywin", [128, D], BF16))
            x2win = en(nc.sbuf_tensor("x2win", [128, D], F32))
            p = Plan(tot)

            with nc.psum_tensor("psG", [128, 4 * R], F32) as psG:
                gu_mm, silu_done, hdone, wld = {}, {}, {}, {}
                for ic in range(ICH):
                    par = ic % 2
                    w = [at("ag_w")]
                    if ic == 0:
                        w.append(at("xn2T_ready"))
                    if ic >= 2:
                        w.append(("cT", gu_mm[ic - 2]))
                    for half, oc in ((0, ic), (1, ICH + ic)):
                        rank, oi = divmod(oc, OCH)
                        off = rank * WSEC_E + oi * (128 * DMC * 128)
                        q = (2 * par + half) * (DMC * 128)
                        wld[(ic, half)] = p.dma(
                            "sync", wstage[:, q:q + DMC * 128],
                            w_full[off:off + 128 * DMC * 128].rearrange(
                                "(q f) -> q f", q=128),
                            waits=w if half == 0 else [])

                    wmm = [("qS", wld[(ic, 1)]), at("xn2T_ready")]
                    if ic >= 2:
                        wmm.append(("cS", silu_done[ic - 2]))
                        wmm.append(("cV", hdone[ic - 2]))

                    def mmGU(e, ic=ic, par=par):
                        for half in range(2):
                            q = (2 * par + half) * (DMC * 128)
                            pbase = (2 * par + half) * R
                            for dmc in range(DMC):
                                for sb in range(2):
                                    ins = e.matmul(
                                        psG[:, pbase + sb * 512:
                                            pbase + sb * 512 + 512],
                                        wstage[:, q + dmc * 128:
                                               q + dmc * 128 + 128],
                                        xnT[:, dmc * R + sb * 512:
                                            dmc * R + sb * 512 + 512],
                                        start=(dmc == 0),
                                        stop=(dmc == DMC - 1))
                        return ins
                    gu_mm[ic] = p.add("tensor", mmGU, waits=wmm, inc="cT")
                    silu_done[ic] = p.add(
                        "scalar",
                        lambda e, par=par: e.activation(
                            out=sgbuf[:],
                            in_=psG[:, 2 * par * R:2 * par * R + R],
                            func=AF.Silu),
                        waits=[("cT", gu_mm[ic])], inc="cS")
                    hdone[ic] = p.add(
                        "vector",
                        lambda e, ic=ic, par=par: e.tensor_tensor(
                            hT[:, ic * R:(ic + 1) * R], sgbuf[:],
                            psG[:, (2 * par + 1) * R:(2 * par + 1) * R + R],
                            ALU.mult),
                        waits=[("cS", silu_done[ic])], inc="cV")
                mark("hT_done", "cV")
                if upto == "D1h":
                    p.add("gpsimd", lambda e: e.nop(),
                          waits=[("qS", tot.get("qS", 0)), ("qG", tot.get("qG", 0)),
                                 ("cV", tot.get("cV", 0)), ("cS", tot.get("cS", 0)),
                                 ("cT", tot.get("cT", 0)), ("cAG", tot.get("cAG", 0))])
                    p.emit(nc, sems)
                    dump_and_finish(src_sbuf=hT[:, 0:16 * R])
                    return nc

            with nc.psum_tensor("psY", [128, 2 * R], F32) as psY:
                dn_mm, yt_ev, ytw = {}, {}, {}
                for dmc in range(DMC):
                    par = dmc % 2
                    rank, dmci = divmod(dmc, 2)
                    off = rank * WSEC_E + GUPART_E + dmci * (ICH * 128 * 128)
                    w = [at("hT_done")]
                    if dmc >= 2:
                        w.append(("cT", dn_mm[dmc - 2]))
                    dnld = p.dma(
                        "sync",
                        dnstage[:, par * (ICH * 128):
                                (par + 1) * (ICH * 128)].rearrange(
                            "q (ic f) -> q ic f", ic=ICH),
                        w_full[off:off + ICH * 128 * 128].rearrange(
                            "(ic q f) -> q ic f", ic=ICH, q=128),
                        waits=w)
                    wmm = [("qS", dnld), at("hT_done")]
                    if dmc >= 2:
                        wmm.append(("cV", yt_ev[dmc - 2]))

                    def mmDN(e, par=par):
                        for ic in range(ICH):
                            for sb in range(2):
                                ins = e.matmul(
                                    psY[:, par * R + sb * 512:
                                        par * R + sb * 512 + 512],
                                    dnstage[:, par * (ICH * 128) + ic * 128:
                                            par * (ICH * 128) + ic * 128 + 128],
                                    hT[:, ic * R + sb * 512:
                                       ic * R + sb * 512 + 512],
                                    start=(ic == 0), stop=(ic == ICH - 1))
                        return ins
                    dn_mm[dmc] = p.add("tensor", mmDN, waits=wmm, inc="cT")
                    yt_ev[dmc] = p.add("vector",
                                       lambda e, par=par: e.tensor_copy(
                                           ytT[:, par * R:par * R + R],
                                           psY[:, par * R:par * R + R]),
                                       waits=[("cT", dn_mm[dmc])], inc="cV")
                    ytw[dmc] = p.dma("gpsimd",
                                     yT_hbm[dmc * 128:(dmc + 1) * 128, :],
                                     ytT[:, par * R:par * R + R],
                                     waits=[("cV", yt_ev[dmc])])
                mark("yT_written", "qG")
                if upto == "D2y":
                    p.add("gpsimd", lambda e: e.nop(),
                          waits=[("qS", tot.get("qS", 0)), ("qG", tot.get("qG", 0)),
                                 ("cV", tot.get("cV", 0)), ("cS", tot.get("cS", 0)),
                                 ("cT", tot.get("cT", 0)), ("cAG", tot.get("cAG", 0))])
                    p.emit(nc, sems)
                    dump_and_finish(src_ap_hbm_bf16=yT_hbm[:])
                    return nc

                ln3_ap, outv, outdma = {}, {}, {}
                for sc in range(SC):
                    w = [at("yT_written")]
                    if sc >= 1:
                        w.append(("cS", ln3_ap[sc - 1]))
                    yld = p.dma("sync", ywin[:],
                                yT_hbm[:, sc * 128:(sc + 1) * 128],
                                waits=w, transpose=True)
                    x2ld = p.dma("sync", x2win[:],
                                 x2_hbm[sc * 128:(sc + 1) * 128, :],
                                 waits=([("cV", outv[sc - 1])] if sc >= 1
                                        else []) + [at("x2_written")])
                    v_s1 = p.add("scalar",
                                 lambda e: e.activation(
                                     out=junk[:], in_=ywin[:], func=AF.Square,
                                     accum_out=stats[:, 66:67]),
                                 waits=[("qS", yld)], inc="cS")
                    v_s2 = p.add("scalar",
                                 lambda e: e.activation(
                                     out=junk[:], in_=ywin[:], func=AF.Copy,
                                     accum_out=stats[:, 67:68]),
                                 waits=[("cS", v_s1)], inc="cS")

                    def smalls(e):
                        mu = stats[:, 68:69]
                        t = stats[:, 69:70]
                        scl = stats[:, 70:71]
                        tmp = stats[:, 71:72]
                        u1 = stats[:, 75:76]
                        e.tensor_scalar_mul(mu, stats[:, 67:68], 1.0 / D)
                        e.tensor_scalar_mul(t, stats[:, 66:67], 1.0 / D)
                        e.tensor_tensor(tmp, mu, mu, ALU.mult)
                        e.tensor_scalar(scl, t, 1e-5, 1e-11, ALU.mult, ALU.add)
                        e.tensor_tensor(u1, t, tmp, ALU.subtract)
                        e.engine_nop()
                        return e.tensor_tensor(stats[:, 71:72], u1, scl,
                                               ALU.add)
                    v_sm = p.add("vector", smalls, waits=[("cS", v_s2)],
                                 inc="cV")
                    v_sq = p.add("scalar",
                                 lambda e: e.activation(out=stats[:, 72:73],
                                                        in_=stats[:, 71:72],
                                                        func=AF.Sqrt),
                                 waits=[("cV", v_sm)], inc="cS")

                    def smalls2(e):
                        rstd = stats[:, 73:74]
                        nmr = stats[:, 74:75]
                        e.reciprocal(rstd, stats[:, 72:73])
                        e.engine_nop()
                        return e.scalar_tensor_tensor(nmr, stats[:, 68:69],
                                                      -1.0, rstd, ALU.mult,
                                                      ALU.mult)
                    v_sm2 = p.add("vector", smalls2, waits=[("cS", v_sq)],
                                  inc="cV")
                    lw = [("cV", v_sm2)]
                    if sc >= 1:
                        lw.append(("cV", outv[sc - 1]))
                    ln3_ap[sc] = p.add("scalar",
                                       lambda e: e.activation(
                                           out=lnout[:], in_=ywin[:],
                                           func=AF.Identity,
                                           scale=stats[:, 73:74],
                                           bias=stats[:, 74:75]),
                                       waits=lw, inc="cS")
                    ow = [("cS", ln3_ap[sc]), ("qS", x2ld)]
                    if sc >= 1:
                        ow.append(("qS", outdma[sc - 1]))
                    outv[sc] = p.add("vector",
                                     lambda e: e.tensor_tensor(
                                         x2win[:], x2win[:], lnout[:],
                                         ALU.add),
                                     waits=ow, inc="cV")
                    outdma[sc] = p.dma("sync",
                                       y_out[sc * 128:(sc + 1) * 128, :],
                                       x2win[:], waits=[("cV", outv[sc])])
                p.add("sync", lambda e: e.nop(), waits=[("qS", tot["qS"])])
                p.emit(nc, sems)

    return nc


_NC_CACHE = None


def kernel(**inputs):
    global _NC_CACHE
    g = {k: np.asarray(v) for k, v in inputs.items()}
    x = np.ascontiguousarray(
        g["L0_attn_x"].reshape(8192, D).astype(np.float32, copy=False))
    wqT = np.ascontiguousarray(g["L0_attn_W_q"].T.astype(np.float32, copy=False))
    wkvT = np.ascontiguousarray(
        g["L0_attn_W_down_kv"].T.astype(np.float32, copy=False))
    wupkT = np.ascontiguousarray(
        np.transpose(g["L0_attn_W_up_k"], (2, 0, 1)).reshape(L, D).astype(
            np.float32, copy=False))
    wupvT = np.ascontiguousarray(
        np.transpose(g["L0_attn_W_up_v"], (2, 0, 1)).reshape(L, D).astype(
            np.float32, copy=False))
    woT = np.ascontiguousarray(g["L0_attn_W_o"].T.astype(np.float32, copy=False))
    gu_u8 = np.ascontiguousarray(g["L0_ffn_gate_up_packed"]).view(
        np.uint8).reshape(2 * I_FF, D // 2)
    gu_scT = np.ascontiguousarray(g["L0_ffn_gate_up_scales"].T)
    dn_u8 = np.ascontiguousarray(g["L0_ffn_down_packed"]).view(
        np.uint8).reshape(D, I_FF // 2)
    dn_scT = np.ascontiguousarray(g["L0_ffn_down_scales"].T)

    if UPTO != "full":
        nc = build_nc(UPTO)
    else:
        if _NC_CACHE is None:
            _NC_CACHE = build_nc()
        nc = _NC_CACHE

    in_maps = []
    for c in range(NCORE):
        in_maps.append({
            "x": np.ascontiguousarray(x[c * R:(c + 1) * R]),
            "wqT": wqT, "wkvT": wkvT, "wupkT": wupkT, "wupvT": wupvT,
            "woT": woT,
            "gu_u8": np.ascontiguousarray(gu_u8[c * OSH:(c + 1) * OSH]),
            "gu_sc": np.ascontiguousarray(gu_scT[c * OSH:(c + 1) * OSH]),
            "dn_u8": np.ascontiguousarray(dn_u8[c * DSH:(c + 1) * DSH]),
            "dn_sc": np.ascontiguousarray(dn_scT[c * DSH:(c + 1) * DSH]),
        })
    res = run_bass_kernel_spmd(nc, in_maps, core_ids=list(range(NCORE)))
    out = np.concatenate([res.results[c]["y"] for c in range(NCORE)], axis=0)
    return out.reshape(1, 8192, D).astype(np.float32, copy=False)


# version 03 (final)
# speedup vs baseline: 1.0047x; 1.0047x over previous
import os
import numpy as np
import ml_dtypes
from contextlib import ExitStack
import concourse.bass as bass
import concourse.mybir as mybir
from concourse.bass_utils import run_bass_kernel_spmd

F32 = mybir.dt.float32
BF16 = mybir.dt.bfloat16
F16 = mybir.dt.float16
U8 = mybir.dt.uint8
AF = mybir.ActivationFunctionType
ALU = mybir.AluOpType
AXX = mybir.AxisListType.X

NCORE = 8
D = 2048
R = 1024
SC = 8
H = 16
DH = 128
L = 512
I_FF = 5632
OSH = 1408
OCH = 11
DSH = 256
ICH = 44
DMC = 16
INV_SQRT_DH = float(2.0 ** -3.5)
EXP_BIAS = -12.0

KPART_E = H * DH * R
VPART_E = R * D
KVSEC_E = KPART_E + VPART_E
GUPART_E = D * OSH
DNPART_E = DSH * I_FF
WSEC_E = GUPART_E + DNPART_E

ENGS = ("sync", "gpsimd", "scalar", "vector", "tensor")
UPTO = "full"


class Plan:
    def __init__(self, tot):
        self.stream = {e: [] for e in ENGS}
        self.tot = tot

    def add(self, eng, fn, waits=None, inc=None, incn=1):
        self.stream[eng].append(
            (fn, [w for w in (waits or []) if w is not None], inc, incn))
        if inc is not None:
            self.tot[inc] = self.tot.get(inc, 0) + incn
            return self.tot[inc]
        return None

    def dma(self, eng, out, in_, waits=None, **kw):
        sem = "qG" if eng == "gpsimd" else "qS"
        return self.add(eng, lambda e: e.dma_start(out=out, in_=in_, **kw),
                        waits, sem, 16)

    def emit(self, nc, sems):
        with nc.Block() as blk:
            for eng_name in ENGS:
                ops = self.stream[eng_name]
                if not ops:
                    continue

                def make_section(ops):
                    def section(e):
                        seen = {}
                        for fn, waits, inc, incn in ops:
                            for sname, val in waits:
                                if val is None or val <= 0:
                                    continue
                                if seen.get(sname, -1) >= val:
                                    continue
                                e.wait_ge(sems[sname], val)
                                seen[sname] = val
                            ins = fn(e)
                            if inc is not None:
                                ins.then_inc(sems[inc], incn)
                    return section

                getattr(blk, eng_name)(make_section(ops))


def build_nc(upto="full"):
    ORDER = {"A0": 0, "A1": 0, "A": 0, "ADQ": 0, "B": 1, "CKV": 1, "CKT": 1,
             "CCT": 1, "CKP": 1, "C": 2, "C2": 3, "D1h": 4, "D2y": 4,
             "full": 4}
    lvl = ORDER[upto]
    skip_xnT = upto == "A0"
    skip_dq = upto in ("A0", "A1")
    nc = bass.Bass("TRN2", target_bir_lowering=False)

    x_in = nc.dram_tensor("x", [R, D], F32, kind="ExternalInput")
    wqT_in = nc.dram_tensor("wqT", [D, D], F32, kind="ExternalInput")
    wkvT_in = nc.dram_tensor("wkvT", [D, L], F32, kind="ExternalInput")
    wupkT_in = nc.dram_tensor("wupkT", [L, D], F32, kind="ExternalInput")
    wupvT_in = nc.dram_tensor("wupvT", [L, D], F32, kind="ExternalInput")
    woT_in = nc.dram_tensor("woT", [D, D], F32, kind="ExternalInput")
    gu_u8_in = nc.dram_tensor("gu_u8", [OSH, D // 2], U8, kind="ExternalInput")
    gu_sc_in = nc.dram_tensor("gu_sc", [OSH, DMC], F16, kind="ExternalInput")
    dn_u8_in = nc.dram_tensor("dn_u8", [DSH, I_FF // 2], U8, kind="ExternalInput")
    dn_sc_in = nc.dram_tensor("dn_sc", [DSH, ICH], F16, kind="ExternalInput")
    y_out = nc.dram_tensor("y", [R, D], F32, kind="ExternalOutput")

    ident_h = nc.inline_tensor(np.eye(128, dtype=ml_dtypes.bfloat16), "ident128")

    xn_hbm = nc.dram_tensor("xn_hbm", [R, D], BF16, kind="Internal")
    xn2_hbm = nc.dram_tensor("xn2_hbm", [R, D], BF16, kind="Internal")
    x2_hbm = nc.dram_tensor("x2_hbm", [R, D], F32, kind="Internal")
    yT_hbm = nc.dram_tensor("yT_hbm", [D, R], BF16, kind="Internal")
    kv_part = nc.dram_tensor("kv_part", [KVSEC_E], BF16, kind="Internal")
    kv_full = nc.dram_tensor("kv_full", [NCORE * KVSEC_E], BF16,
                             kind="Internal")
    w_part = nc.dram_tensor("w_part", [WSEC_E], BF16, kind="Internal")
    w_full = nc.dram_tensor("w_full", [NCORE * WSEC_E], BF16,
                            kind="Internal")

    tot = {}
    ck = {}

    def mark(name, sem):
        ck[name] = (sem, tot.get(sem, 0))

    def at(name):
        return ck[name]

    with ExitStack() as _st0:
        en = _st0.enter_context
        _qS = en(nc.semaphore("qS"))
        _qG = en(nc.semaphore("qG"))
        _cV = en(nc.semaphore("cV"))
        _cS = en(nc.semaphore("cS"))
        _cT = en(nc.semaphore("cT"))
        _cAG = en(nc.semaphore("cAG"))
        xnT = en(nc.sbuf_tensor("xnT", [128, DMC * R], BF16))
        ident = en(nc.sbuf_tensor("ident", [128, 128], BF16))
        stats = en(nc.sbuf_tensor("stats", [128, 80], F32))
        junk = en(nc.sbuf_tensor("junk", [128, D], F32))
        lnout = en(nc.sbuf_tensor("lnout", [128, D], F32))
        consts = en(nc.sbuf_tensor("consts", [128, 4], F32))
        CONSTVALS = (1e-5, -4.0, -6.0, EXP_BIAS)
        for _i, _v in enumerate(CONSTVALS):
            nc.const_aps.register(F32, _v, consts[:, _i:_i + 1])
        
        sems = {"qS": _qS, "qG": _qG, "cV": _cV, "cS": _cS, "cT": _cT,
                "cAG": _cAG}

        def plan_ln(p, sc, src, dst, ready_waits, eps=1e-5, apply_waits=None):
            v_sq = p.add("scalar",
                         lambda e: e.activation(out=junk[:], in_=src,
                                                func=AF.Square,
                                                accum_out=stats[:, 8 + sc:9 + sc]),
                         waits=ready_waits, inc="cS")
            v_red = p.add("vector",
                          lambda e: e.reduce_sum(stats[:, 0 + sc:1 + sc], src,
                                                 axis=AXX),
                          waits=ready_waits, inc="cV")

            def small(e):
                mu = stats[:, 16 + sc:17 + sc]
                var = stats[:, 24 + sc:25 + sc]
                tmp = stats[:, 64:65]
                e.tensor_scalar_mul(mu, stats[:, 0 + sc:1 + sc], 1.0 / D)
                e.tensor_scalar_mul(var, stats[:, 8 + sc:9 + sc], 1.0 / D)
                e.tensor_tensor(tmp, mu, mu, ALU.mult)
                e.engine_nop()
                return e.tensor_tensor(var, var, tmp, ALU.subtract)
            v_var = p.add("vector", small, waits=[("cS", v_sq), ("cV", v_red)],
                          inc="cV")
            v_std = p.add("scalar",
                          lambda e: e.activation(out=stats[:, 48 + sc:49 + sc],
                                                 in_=stats[:, 24 + sc:25 + sc],
                                                 func=AF.Sqrt, bias=eps),
                          waits=[("cV", v_var)], inc="cS")

            def small2(e):
                rstd = stats[:, 32 + sc:33 + sc]
                nmr = stats[:, 40 + sc:41 + sc]
                e.reciprocal(rstd, stats[:, 48 + sc:49 + sc])
                e.engine_nop()
                return e.scalar_tensor_tensor(nmr, stats[:, 16 + sc:17 + sc],
                                              -1.0, rstd, ALU.mult, ALU.mult)
            v_rs = p.add("vector", small2, waits=[("cS", v_std)], inc="cV")
            v_apv = p.add("vector",
                          lambda e: e.tensor_scalar(
                              dst, src, stats[:, 32 + sc:33 + sc],
                              stats[:, 40 + sc:41 + sc], ALU.mult, ALU.add),
                          waits=(apply_waits or []), inc="cV")
            return p.add("scalar", lambda e: e.nop(),
                         waits=[("cV", v_apv)], inc="cS")

        def dump_and_finish(src_ap_hbm_bf16=None, src_sbuf=None):
            p2 = Plan(tot)
            if src_ap_hbm_bf16 is not None:
                n = 1
                for s in src_ap_hbm_bf16.shape:
                    n *= s
                dst = y_out[:].flatten()[0:n]
                for s in reversed(src_ap_hbm_bf16.shape[1:]):
                    pass
                dst = dst.rearrange("(a b) -> a b", a=src_ap_hbm_bf16.shape[0])
                if len(src_ap_hbm_bf16.shape) == 3:
                    dst = dst.rearrange("a (b c) -> a b c",
                                        b=src_ap_hbm_bf16.shape[1])
                p2.dma("gpsimd", dst, src_ap_hbm_bf16, waits=[])
            else:
                Fd = src_sbuf.shape[1]
                n = 128 * Fd
                dst = y_out[:].flatten()[0:n].rearrange("(q f) -> q f", q=128)
                p2.dma("gpsimd", dst, src_sbuf, waits=[])
            p2.add("gpsimd", lambda e: e.nop(), waits=[("qG", tot["qG"])])
            p2.emit(nc, sems)

        with ExitStack() as _stA:
            en = _stA.enter_context
            xwin = en(nc.sbuf_tensor("xwin", [128, 2 * D], F32))
            xnwin = en(nc.sbuf_tensor("xnwin", [128, 2 * D], BF16))
            bwin = en(nc.sbuf_tensor("bwin", [128, 2 * 1408], U8))
            nlo_u8 = en(nc.sbuf_tensor("nlo", [128, 1408], U8))
            nhi_u8 = en(nc.sbuf_tensor("nhi", [128, 1408], U8))
            nf2 = [en(nc.sbuf_tensor(f"nf{pp}", [128, 1408], F32))
                   for pp in range(2)]
            ge2 = [en(nc.sbuf_tensor(f"ge{pp}", [128, 1408], F32))
                   for pp in range(2)]
            m2 = [en(nc.sbuf_tensor(f"m{pp}", [128, 1408], F32))
                  for pp in range(2)]
            sg2 = [en(nc.sbuf_tensor(f"sg{pp}", [128, 1408], F32))
                   for pp in range(2)]
            a2 = [en(nc.sbuf_tensor(f"a{pp}", [128, 1408], F32))
                  for pp in range(2)]
            b2 = [en(nc.sbuf_tensor(f"b{pp}", [128, 1408], F32))
                  for pp in range(2)]
            winter = en(nc.sbuf_tensor("winter", [128, I_FF], BF16))
            tstage = en(nc.sbuf_tensor("tstage", [128, ICH * 128], BF16))
            scw = en(nc.sbuf_tensor("scw", [128, ICH], F32))
            scw16 = en(nc.sbuf_tensor("scw16", [128, ICH], F16))
            pstr_t = [en(nc.psum_tensor(f"pstr{pp}", [128, 128], BF16))
                      for pp in range(2)]
            p = Plan(tot)
            v_consts = p.add("vector",
                             lambda e: [e.memset(consts[:, _i:_i + 1], _v)
                                        for _i, _v in enumerate(CONSTVALS)][-1],
                             inc="cV")
            mark("consts", "cV")
            ld_id = p.dma("sync", ident[:], ident_h[:])

            xload, xn_done = {}, {}
            for sc in range(SC):
                w = [("cS", xn_done[sc - 2])] if sc >= 2 else []
                xload[sc] = p.dma("sync", xwin[:, (sc % 2) * D:(sc % 2) * D + D],
                                  x_in[sc * 128:(sc + 1) * 128, :], waits=w)
                src = xwin[:, (sc % 2) * D:(sc % 2) * D + D]
                dst = xnwin[:, (sc % 2) * D:(sc % 2) * D + D]
                xn_done[sc] = plan_ln(p, sc, src, dst, [("qS", xload[sc])])
                p.dma("gpsimd", xn_hbm[sc * 128:(sc + 1) * 128, :], dst,
                      waits=[("cS", xn_done[sc])])
            mark("xn_written", "qG")

            if not skip_xnT:
                for dmc in range(DMC):
                    p.dma("sync", xnT[:, dmc * R:(dmc + 1) * R],
                          xn_hbm[:, dmc * 128:(dmc + 1) * 128],
                          waits=[at("xn_written")], transpose=True)
            mark("xnT_ready", "qS")

            dq_chunks = [("gu", i) for i in range(OCH)] + \
                        [("dn", i) for i in range(4)]
            if skip_dq:
                dq_chunks = []
            _dqn = os.environ.get("DQN")
            if _dqn is not None:
                dq_chunks = dq_chunks[:int(_dqn)]
            tr_done, nib_done, ev_done, sc_used = {}, {}, {}, {}
            tp = 0
            for ci, (kind, i) in enumerate(dq_chunks):
                F = 1024 if kind == "gu" else 1408
                ngrp = DMC if kind == "gu" else ICH
                w = [("cV", nib_done[ci - 2])] if ci >= 2 else []
                bw = bwin[:, (ci % 2) * 1408:(ci % 2) * 1408 + F]
                if kind == "gu":
                    bsrc = gu_u8_in[i * 128:(i + 1) * 128, :]
                    ssrc = gu_sc_in[i * 128:(i + 1) * 128, :]
                else:
                    dmci, half = divmod(i, 2)
                    bsrc = dn_u8_in[dmci * 128:(dmci + 1) * 128,
                                    half * 1408:(half + 1) * 1408]
                    ssrc = dn_sc_in[dmci * 128:(dmci + 1) * 128, :]
                p.dma("sync", bw, bsrc, waits=w)
                scf = [("cV", sc_used[ci - 1])] if ci >= 1 else []
                p.dma("sync", scw16[:, 0:ngrp], ssrc, waits=scf)
                ldv = tot["qS"]

                v_cs = p.add("vector",
                             lambda e, ngrp=ngrp: e.tensor_copy(
                                 scw[:, 0:ngrp], scw16[:, 0:ngrp]),
                             waits=[("qS", ldv)], inc="cV")
                sc_used[ci] = v_cs

                v_ext = p.add("vector",
                              lambda e, bw=bw, F=F: (
                                  e.tensor_scalar(nlo_u8[:, 0:F], bw, 15,
                                                  None, ALU.bitwise_and),
                                  e.tensor_scalar(nhi_u8[:, 0:F], bw, 4, None,
                                                  ALU.logical_shift_right)
                              )[-1], inc="cV")

                if kind == "gu":
                    woff, wlen = (ci % 2) * D, D
                    gbase, ga = 0, DMC
                    wi_wait = [("cT", tr_done[ci - 2])] if ci >= 2 else []
                else:
                    woff, wlen = (i % 2) * 2816, 2816
                    gbase, ga = (i % 2) * 22, 22
                    wi_wait = [("cT", tr_done[10 if i < 2 else 12])]

                def stage1(e, F=F):
                    e.tensor_copy(nf2[0][:, 0:F], nlo_u8[:, 0:F])
                    e.tensor_copy(nf2[1][:, 0:F], nhi_u8[:, 0:F])
                    e.tensor_scalar(ge2[0][:, 0:F], nf2[0][:, 0:F], 8.0, None,
                                    ALU.is_ge)
                    e.tensor_scalar(ge2[1][:, 0:F], nf2[1][:, 0:F], 8.0, None,
                                    ALU.is_ge)
                    e.scalar_tensor_tensor(m2[0][:, 0:F], ge2[0][:, 0:F], -8.0,
                                           nf2[0][:, 0:F], ALU.mult, ALU.add)
                    e.scalar_tensor_tensor(m2[1][:, 0:F], ge2[1][:, 0:F], -8.0,
                                           nf2[1][:, 0:F], ALU.mult, ALU.add)
                    e.tensor_scalar(sg2[0][:, 0:F], ge2[0][:, 0:F], -2.0, 1.0,
                                    ALU.mult, ALU.add)
                    return e.tensor_scalar(sg2[1][:, 0:F], ge2[1][:, 0:F],
                                           -2.0, 1.0, ALU.mult, ALU.add)
                vm = p.add("vector", stage1, waits=[("cV", v_ext)], inc="cV")
                vr = p.add("scalar",
                           lambda e, F=F: (
                               e.activation(out=a2[0][:, 0:F],
                                            in_=m2[0][:, 0:F],
                                            func=AF.Relu, bias=-4.0),
                               e.activation(out=a2[1][:, 0:F],
                                            in_=m2[1][:, 0:F],
                                            func=AF.Relu, bias=-4.0),
                               e.activation(out=b2[0][:, 0:F],
                                            in_=m2[0][:, 0:F],
                                            func=AF.Relu, bias=-6.0),
                               e.activation(out=b2[1][:, 0:F],
                                            in_=m2[1][:, 0:F],
                                            func=AF.Relu, bias=-6.0))[-1],
                           waits=[("cV", vm)], inc="cS")

                def stage2(e, F=F, gbase=gbase, ga=ga, woff=woff, wlen=wlen):
                    for pp in range(2):
                        e.tensor_tensor(ge2[pp][:, 0:F], m2[pp][:, 0:F],
                                        a2[pp][:, 0:F], ALU.add)
                    for pp in range(2):
                        e.scalar_tensor_tensor(nf2[pp][:, 0:F],
                                               ge2[pp][:, 0:F], 0.5,
                                               b2[pp][:, 0:F],
                                               ALU.mult, ALU.add)
                    for pp in range(2):
                        e.tensor_tensor(a2[pp][:, 0:F], nf2[pp][:, 0:F],
                                        sg2[pp][:, 0:F], ALU.mult)
                    last = None
                    for pp in range(2):
                        scb = scw[:, gbase:gbase + ga].unsqueeze(
                            2).broadcast_to([128, ga, 64])
                        sv3 = a2[pp][:, 0:F].rearrange("q (a b) -> q a b",
                                                       a=ga)
                        wout = winter[:, woff:woff + wlen].rearrange(
                            "q (a b two) -> q a b two", a=ga,
                            two=2)[:, :, :, pp]
                        last = e.tensor_tensor(wout, sv3, scb, ALU.mult)
                    return last
                vw2 = p.add("vector", stage2,
                            waits=[("cS", vr), ("cV", v_cs)] + wi_wait,
                            inc="cV")
                nib_done[ci] = vw2

                if kind == "dn" and i % 2 == 0:
                    continue
                nblk = DMC if kind == "gu" else ICH
                base = woff if kind == "gu" else 0
                for b in range(nblk):
                    tw = [("cV", vw2), ("qS", ld_id)]
                    if tp >= 2:
                        tw.append(("cV", ev_done[tp - 2]))
                    vt = p.add("tensor",
                               lambda e, b=b, tp=tp, base=base: e.transpose(
                                   pstr_t[tp % 2][:],
                                   winter[:, base + b * 128:base + b * 128 + 128],
                                   ident[:]),
                               waits=tw, inc="cT")
                    ev_done[tp] = p.add(
                        "vector",
                        lambda e, b=b, tp=tp: e.tensor_copy(
                            tstage[:, b * 128:(b + 1) * 128],
                            pstr_t[tp % 2][:]),
                        waits=[("cT", vt)], inc="cV")
                    tp += 1
                tr_done[ci] = tot["cT"]
                stage_done = tot["cV"]

                if kind == "gu":
                    off = i * (128 * DMC * 128)
                    dst = w_part[off:off + 128 * DMC * 128].rearrange(
                        "(q f) -> q f", q=128)
                    p.dma("gpsimd", dst, tstage[:, 0:DMC * 128],
                          waits=[("cV", stage_done)])
                else:
                    dmci = i // 2
                    off = GUPART_E + dmci * (ICH * 128 * 128)
                    dst = w_part[off:off + ICH * 128 * 128].rearrange(
                        "(ic q f) -> q ic f", ic=ICH, q=128)
                    src = tstage[:].rearrange("q (ic f) -> q ic f", ic=ICH)
                    p.dma("gpsimd", dst, src, waits=[("cV", stage_done)])
            mark("w_part_written", "qG")
            if ev_done:
                p.add("tensor", lambda e: e.nop(),
                      waits=[("cV", max(ev_done.values()))])
            if lvl == 0:
                p.add("gpsimd", lambda e: e.nop(),
                      waits=[("qS", tot.get("qS", 0)), ("qG", tot.get("qG", 0)),
                             ("cV", tot.get("cV", 0)), ("cS", tot.get("cS", 0)),
                             ("cT", tot.get("cT", 0))])
            p.emit(nc, sems)
            if lvl == 0:
                if upto == "ADQ":
                    dump_and_finish(src_ap_hbm_bf16=w_part[0:R * D].rearrange(
                        "(q f) -> q f", q=R))
                else:
                    dump_and_finish(src_ap_hbm_bf16=xn_hbm[:])
                return nc

        with nc.sbuf_tensor("attoT", [128, H * R], BF16) as attoT:
            with nc.sbuf_tensor("qT", [128, H * R], BF16) as qT:
                with ExitStack() as _stB:
                    en = _stB.enter_context
                    wbuf = en(nc.sbuf_tensor("wbuf", [128, 16 * 1024], BF16))
                    cTsb = en(nc.sbuf_tensor("cTsb", [128, 4 * R], BF16))
                    kvT = en(nc.sbuf_tensor("kvT", [128, H * R], BF16))
                    vstage = en(nc.sbuf_tensor("vstage", [128, 2 * D], BF16))
                    psB = en(nc.psum_tensor("psB", [128, 2 * 512], F32))
                    p = Plan(tot)
                    for dmc in range(DMC):
                        p.dma("gpsimd", wbuf[:, dmc * 512:(dmc + 1) * 512],
                              wkvT_in[dmc * 128:(dmc + 1) * 128, :])
                    mark("w_kv", "qG")
                    for lc in range(4):
                        p.dma("gpsimd",
                              wbuf[:, 8192 + lc * D:8192 + (lc + 1) * D],
                              wupkT_in[lc * 128:(lc + 1) * 128, :])
                    mark("w_upk", "qG")

                    groups = []
                    for lc in range(4):
                        for sb in range(2):
                            groups.append((
                                [wbuf[:, dmc * 512 + lc * 128:
                                      dmc * 512 + lc * 128 + 128]
                                 for dmc in range(DMC)],
                                [xnT[:, dmc * R + sb * 512:
                                     dmc * R + sb * 512 + 512]
                                 for dmc in range(DMC)],
                                cTsb[:, lc * R + sb * 512:
                                     lc * R + sb * 512 + 512],
                                [at("w_kv"), at("xnT_ready")], ("c", lc, sb)))
                    NCG = len(groups)
                    for h in range(H):
                        for sb in range(2):
                            groups.append((
                                [wbuf[:, 8192 + lc * D + h * 128:
                                      8192 + lc * D + h * 128 + 128]
                                 for lc in range(4)],
                                [cTsb[:, lc * R + sb * 512:
                                      lc * R + sb * 512 + 512]
                                 for lc in range(4)],
                                kvT[:, h * R + sb * 512:h * R + sb * 512 + 512],
                                [at("w_upk")], ("k", h, sb)))
                    for tc in range(SC):
                        for nq in range(4):
                            groups.append((
                                [cTsb[:, lc * R + tc * 128:
                                      lc * R + tc * 128 + 128]
                                 for lc in range(4)],
                                [wbuf[:, lc * D + nq * 512:
                                      lc * D + nq * 512 + 512]
                                 for lc in range(4)],
                                vstage[:, (tc % 2) * D + nq * 512:
                                       (tc % 2) * D + nq * 512 + 512],
                                ["w_upv"], ("v", tc, nq)))

                    ev, vwrite = {}, {}

                    def plan_group(gi, lhs, rhs, dst, extra, tag):
                        par = gi % 2
                        w = [at(x) if isinstance(x, str) else x for x in extra]
                        if gi >= 2:
                            w.append(("cS", ev[gi - 2]))
                        if tag == ("k", 0, 0):
                            w.append(("cS", ev[NCG - 1]))
                        if tag[0] == "v" and tag[1] >= 2 and tag[2] == 0:
                            w.append(("qS", vwrite[tag[1] - 2]))

                        def mm(e, lhs=lhs, rhs=rhs, par=par):
                            nk = len(lhs)
                            for k in range(nk):
                                ins = e.matmul(psB[:, par * 512:par * 512 + 512],
                                               lhs[k], rhs[k],
                                               start=(k == 0),
                                               stop=(k == nk - 1))
                            return ins
                        vt = p.add("tensor", mm, waits=w, inc="cT")
                        ev[gi] = p.add(
                            "scalar",
                            lambda e, dst=dst, par=par: e.copy(
                                out=dst, in_=psB[:, par * 512:par * 512 + 512]),
                            waits=[("cT", vt)], inc="cS")
                        if tag[0] == "v" and tag[2] == 3:
                            tc = tag[1]
                            dst_d = kv_part[KPART_E + tc * 128 * D:
                                            KPART_E + (tc + 1) * 128 * D]
                            dst_d = dst_d.rearrange("(hh t dv) -> t hh dv",
                                                    hh=H, t=128)
                            src_s = vstage[:, (tc % 2) * D:(tc % 2) * D + D]
                            src_s = src_s.rearrange("t (hh dv) -> t hh dv", hh=H)
                            vwrite[tc] = p.dma("sync", dst_d, src_s,
                                               waits=[("cS", ev[gi])])
                    for gi in range(NCG + 2 * H):
                        plan_group(gi, *groups[gi])
                    for lc in range(4):
                        p.dma("gpsimd",
                              wbuf[:, lc * D:(lc + 1) * D],
                              wupvT_in[lc * 128:(lc + 1) * 128, :],
                              waits=[("cS", ev[NCG - 1])] if lc == 0 else [])
                    mark("w_upv", "qG")
                    for gi in range(NCG + 2 * H, len(groups)):
                        plan_group(gi, *groups[gi])
                    kev = ev[NCG + 2 * H - 1]
                    p.dma("sync",
                          kv_part[0:KPART_E].rearrange("(hh d t) -> d hh t",
                                                       hh=H, d=128),
                          kvT[:].rearrange("d (hh t) -> d hh t", hh=H),
                          waits=[("cS", kev)])
                    mark("kwrite", "qS")
                    b_mm_done = ev[len(groups) - 1]
                    mark("b_mm_done", "cS")

                    qev = {}
                    for half in range(2):
                        hw = [("cS", b_mm_done)] if half == 0 else \
                             [("cS", qev[15])]
                        for dmc in range(DMC):
                            p.dma("gpsimd",
                                  wbuf[:, dmc * 1024:(dmc + 1) * 1024],
                                  wqT_in[dmc * 128:(dmc + 1) * 128,
                                         half * 1024:(half + 1) * 1024],
                                  waits=hw if dmc == 0 else [])
                        mark(f"wq{half}", "qG")
                        for dqc8 in range(8):
                            dq = half * 8 + dqc8
                            for sb in range(2):
                                gi2 = dq * 2 + sb
                                par = gi2 % 2
                                w = [at(f"wq{half}")]
                                if gi2 >= 2:
                                    w.append(("cS", qev[gi2 - 2]))
                                else:
                                    w.append(("cS", b_mm_done))

                                def mmq(e, dqc8=dqc8, sb=sb, par=par):
                                    for dmc in range(DMC):
                                        ins = e.matmul(
                                            psB[:, par * 512:par * 512 + 512],
                                            wbuf[:, dmc * 1024 + dqc8 * 128:
                                                 dmc * 1024 + dqc8 * 128 + 128],
                                            xnT[:, dmc * R + sb * 512:
                                                dmc * R + sb * 512 + 512],
                                            start=(dmc == 0),
                                            stop=(dmc == DMC - 1))
                                    return ins
                                vt = p.add("tensor", mmq, waits=w, inc="cT")
                                qev[gi2] = p.add(
                                    "scalar",
                                    lambda e, dq=dq, sb=sb, par=par: e.copy(
                                        out=qT[:, dq * R + sb * 512:
                                               dq * R + sb * 512 + 512],
                                        in_=psB[:, par * 512:par * 512 + 512]),
                                    waits=[("cT", vt)], inc="cS")
                    for h in range(H):
                        ck[f"qTh{h}"] = ("cS", qev[2 * h + 1])
                    mark("qT_done", "cS")
                    if lvl == 1:
                        p.add("gpsimd", lambda e: e.nop(),
                              waits=[("qS", tot.get("qS", 0)), ("qG", tot.get("qG", 0)),
                                     ("cV", tot.get("cV", 0)), ("cS", tot.get("cS", 0)),
                                     ("cT", tot.get("cT", 0)), ("cAG", tot.get("cAG", 0))])
                        p.emit(nc, sems)
                        if upto == "CKP":
                            dump_and_finish(src_ap_hbm_bf16=kv_part[
                                0:KPART_E].rearrange("(hh d t) -> d hh t",
                                                     hh=H, d=128))
                        elif upto == "CKT":
                            dump_and_finish(src_sbuf=kvT[:])
                        elif upto == "CCT":
                            dump_and_finish(src_sbuf=cTsb[:])
                        elif upto == "CKV":
                            dump_and_finish(src_ap_hbm_bf16=kv_full[
                                0:KPART_E].rearrange("(hh d t) -> d hh t",
                                                     hh=H, d=128))
                        else:
                            dump_and_finish(src_sbuf=qT[:])
                        return nc

                    agw = [at("kwrite")] + \
                          [("qS", vwrite[tc]) for tc in range(SC)]
                    p.add("gpsimd",
                          lambda e: e.collective_compute(
                              "AllGather", ALU.bypass,
                              replica_groups=[list(range(NCORE))],
                              ins=[kv_part[:]], outs=[kv_full[:]]),
                          waits=agw, inc="cAG")
                    mark("ag_kv", "cAG")
                    if os.environ.get("SKIP_AGW"):
                        p.add("gpsimd", lambda e: e.nop(),
                              waits=[at("w_part_written")], inc="cAG")
                    else:
                        p.add("gpsimd",
                              lambda e: e.collective_compute(
                                  "AllGather", ALU.bypass,
                                  replica_groups=[list(range(NCORE))],
                                  ins=[w_part[:]], outs=[w_full[:]]),
                              waits=[at("w_part_written")], inc="cAG")
                    mark("ag_w", "cAG")
                    p.emit(nc, sems)

                with ExitStack() as _stC:
                    en = _stC.enter_context
                    kbuf = en(nc.sbuf_tensor("kbuf", [128, 2 * R], BF16))
                    vbuf = en(nc.sbuf_tensor("vbuf", [128, 2 * R], BF16))
                    pT = en(nc.sbuf_tensor("pT", [128, 2 * R], BF16))
                    accbA = en(nc.sbuf_tensor("accbA", [128, R], BF16))
                    accbB = en(nc.sbuf_tensor("accbB", [128, R], BF16))
                    accfA = en(nc.sbuf_tensor("accfA", [128, R], F32))
                    accfB = en(nc.sbuf_tensor("accfB", [128, R], F32))
                    accs = en(nc.sbuf_tensor("accs", [128, R], BF16))
                    onecol = en(nc.sbuf_tensor("onecol", [128, 1], BF16))
                    onerow = en(nc.sbuf_tensor("onerow", [1, 128], BF16))
                    rDrow = en(nc.sbuf_tensor("rDrow", [1, R], F32))
                    rDrowb = en(nc.sbuf_tensor("rDrowb", [1, R], BF16))
                    bcsb = en(nc.sbuf_tensor("bcsb", [128, R], F32))
                    psS0 = en(nc.psum_tensor("psS0", [128, R], F32))
                    psS1 = en(nc.psum_tensor("psS1", [128, R], F32))
                    psO0 = en(nc.psum_tensor("psO0", [128, R], F32))
                    psO1 = en(nc.psum_tensor("psO1", [128, R], F32))
                    p = Plan(tot)
                    psS = (psS0, psS1)
                    psO = (psO0, psO1)

                    v_ones = p.add("vector",
                                   lambda e: (e.memset(onecol[:], 1.0),
                                              e.memset(onerow[:], 1.0))[-1],
                                   waits=[at("qT_done")], inc="cV")

                    kv_ld, pv_slot, pv_chunk = {}, {}, {}
                    exp_done, acc_done, evac_done, recip_done = {}, {}, {}, {}

                    HEADS = list(range(H))
                    if os.environ.get("DBG_REVHEADS"):
                        HEADS = HEADS[::-1]
                    for hi in range(H):
                        h = HEADS[hi]
                        for r in range(NCORE):
                            u = hi * NCORE + r
                            w = [at("ag_w")]
                            if u == 0:
                                w.append(at("qT_done"))
                            if u >= 2:
                                w.append(("cT", pv_slot[u - 2]))
                            koff = r * KVSEC_E + h * (DH * R)
                            p.dma("sync",
                                  kbuf[:, (u % 2) * R:(u % 2) * R + R],
                                  kv_full[koff:koff + DH * R].rearrange(
                                      "(d t) -> d t", d=128),
                                  waits=w)
                            voff = r * KVSEC_E + KPART_E
                            src = kv_full[voff:voff + R * D].rearrange(
                                "(tc hh t dv) -> tc hh t dv", tc=SC, hh=H,
                                t=128)
                            src = src[:, h:h + 1].squeeze(1).transpose([1, 0, 2])
                            kv_ld[u] = p.dma(
                                "sync",
                                vbuf[:, (u % 2) * R:(u % 2) * R + R].rearrange(
                                    "t (tc dv) -> t tc dv", tc=SC),
                                src, waits=[])
                            for j in range(NCORE):
                                g = u * NCORE + j
                                par = g % 2
                                first = (r == 0 and j == 0)
                                w = [("qS", kv_ld[u]), ("cV", v_ones),
                                     at(f"qTh{h}") if (r == 0 and j == 0)
                                     else None, at("consts")]
                                if hi == 0 and first:
                                    w.append(at("qT_done"))
                                if g >= 2:
                                    w.append(("cS", exp_done[g - 2]))
                                if first and hi >= 1:
                                    w.append(("cV", recip_done[hi - 1]))
                                if r == 0 and j == 1 and hi >= 1:
                                    w.append(("cV", evac_done[hi - 1]))

                                def mmS(e, u=u, j=j, h=h, par=par):
                                    kt = kbuf[:, (u % 2) * R + j * 128:
                                              (u % 2) * R + j * 128 + 128]
                                    for sb in range(2):
                                        ins = e.matmul(
                                            psS[par][:, sb * 512:sb * 512 + 512],
                                            kt,
                                            qT[:, h * R + sb * 512:
                                               h * R + sb * 512 + 512],
                                            start=True, stop=True)
                                    return ins
                                v_s = p.add("tensor", mmS, waits=w, inc="cT")

                                we = [("cT", v_s)]
                                if g >= 2:
                                    we.append(("cT", pv_chunk[g - 2]))
                                    we.append(("cV", acc_done[g - 2]))
                                exp_done[g] = p.add(
                                    "scalar",
                                    lambda e, par=par: e.activation(
                                        out=pT[:, par * R:par * R + R],
                                        in_=psS[par][:], func=AF.Exp,
                                        scale=INV_SQRT_DH, bias=EXP_BIAS),
                                    waits=we, inc="cS")

                                def accop(e, j=j, par=par, r=r):
                                    src = pT[:, par * R:par * R + R]
                                    acc = accbA if j % 2 == 0 else accbB
                                    if j <= 1:
                                        ret = e.tensor_copy(acc[:], src)
                                    else:
                                        ret = e.tensor_tensor(acc[:], acc[:],
                                                              src, ALU.add)
                                    if j == NCORE - 1:
                                        if r == 0:
                                            e.tensor_copy(accfA[:], accbA[:])
                                            ret = e.tensor_copy(accfB[:],
                                                                accbB[:])
                                        else:
                                            e.tensor_tensor(accfA[:], accfA[:],
                                                            accbA[:], ALU.add)
                                            ret = e.tensor_tensor(
                                                accfB[:], accfB[:], accbB[:],
                                                ALU.add)
                                    return ret
                                acc_done[g] = p.add(
                                    "vector", accop,
                                    waits=[("cS", exp_done[g])], inc="cV")

                                wp = [("cS", exp_done[g])]
                                if first and hi >= 2:
                                    wp.append(("cV", evac_done[hi - 2]))

                                def mmPV(e, u=u, j=j, par=par, parO=hi % 2,
                                         first=first,
                                         last=(r == NCORE - 1 and
                                               j == NCORE - 1)):
                                    vt = vbuf[:, (u % 2) * R + j * 128:
                                              (u % 2) * R + j * 128 + 128]
                                    for sb in range(2):
                                        ins = e.matmul(
                                            psO[parO][:, sb * 512:
                                                      sb * 512 + 512],
                                            vt,
                                            pT[:, par * R + sb * 512:
                                               par * R + sb * 512 + 512],
                                            start=first, stop=last)
                                    return ins
                                v_pv = p.add("tensor", mmPV, waits=wp, inc="cT")
                                pv_chunk[g] = v_pv
                                if j == NCORE - 1:
                                    pv_slot[u] = v_pv

                        if os.environ.get("DBG_PUREPV"):
                            evac_done[hi] = p.add(
                                "vector",
                                lambda e, h=h, hi=hi: e.tensor_copy(
                                    attoT[:, h * R:(h + 1) * R],
                                    psO[hi % 2][:]),
                                waits=[("cT", pv_slot[hi * NCORE + 7])],
                                inc="cV")
                            recip_done[hi] = tot["cV"]
                            continue
                        v_as = p.add(
                            "vector",
                            lambda e: e.tensor_tensor(accs[:], accfA[:],
                                                      accfB[:], ALU.add),
                            waits=[("cV", acc_done[(hi * NCORE + 7) * NCORE + 7])],
                            inc="cV")
                        v_dm = p.add(
                            "tensor",
                            lambda e: [e.matmul(
                                psS0[0:1, sb * 512:sb * 512 + 512], onecol[:],
                                accs[:, sb * 512:sb * 512 + 512],
                                start=True, stop=True) for sb in range(2)][-1],
                            waits=[("cV", v_as),
                                   ("cS", exp_done[(hi * NCORE + 7) * NCORE + 6])],
                            inc="cT")
                        v_rc = p.add("vector",
                                     lambda e: e.reciprocal(rDrow[:],
                                                            psS0[0:1, :]),
                                     waits=[("cT", v_dm)], inc="cV")
                        recip_done[hi] = v_rc
                        v_rb = p.add("vector",
                                     lambda e: e.tensor_copy(rDrowb[:],
                                                             rDrow[:]),
                                     waits=[("cV", v_rc)], inc="cV")
                        v_bc = p.add(
                            "tensor",
                            lambda e: [e.matmul(
                                psS1[:, sb * 512:sb * 512 + 512], onerow[:],
                                rDrowb[0:1, sb * 512:sb * 512 + 512],
                                start=True, stop=True) for sb in range(2)][-1],
                            waits=[("cV", v_rb)], inc="cT")
                        v_cpb = p.add("vector",
                                      lambda e: e.tensor_copy(bcsb[:],
                                                              psS1[:]),
                                      waits=[("cT", v_bc)], inc="cV")
                        if os.environ.get("DBG_DUMPD"):
                            evac_done[hi] = p.add(
                                "vector",
                                lambda e, h=h: e.tensor_copy(
                                    attoT[:, h * R:(h + 1) * R], bcsb[:]),
                                waits=[("cV", v_cpb),
                                       ("cT", pv_slot[hi * NCORE + 7])],
                                inc="cV")
                        elif os.environ.get("DBG_NODIV"):
                            evac_done[hi] = p.add(
                                "vector",
                                lambda e, h=h, hi=hi: e.tensor_copy(
                                    attoT[:, h * R:(h + 1) * R],
                                    psO[hi % 2][:]),
                                waits=[("cV", v_cpb),
                                       ("cT", pv_slot[hi * NCORE + 7])],
                                inc="cV")
                        else:
                            evac_done[hi] = p.add(
                                "vector",
                                lambda e, h=h, hi=hi: e.tensor_tensor(
                                    attoT[:, h * R:(h + 1) * R],
                                    psO[hi % 2][:], bcsb[:], ALU.mult),
                                waits=[("cV", v_cpb),
                                       ("cT", pv_slot[hi * NCORE + 7])],
                                inc="cV")
                    mark("atto_done", "cV")
                    if lvl == 2:
                        p.add("gpsimd", lambda e: e.nop(),
                              waits=[("qS", tot.get("qS", 0)), ("qG", tot.get("qG", 0)),
                                     ("cV", tot.get("cV", 0)), ("cS", tot.get("cS", 0)),
                                     ("cT", tot.get("cT", 0)), ("cAG", tot.get("cAG", 0))])
                        p.emit(nc, sems)
                        if os.environ.get("DBG_DUMPQ"):
                            dump_and_finish(src_sbuf=qT[:])
                        else:
                            dump_and_finish(src_sbuf=attoT[:])
                        return nc
                    p.emit(nc, sems)

            with ExitStack() as _stC2:
                en = _stC2.enter_context
                woT_sb = en(nc.sbuf_tensor("woT_sb", [128, DMC * D], BF16))
                aproj = en(nc.sbuf_tensor("aproj", [128, 2 * D], F32))
                xres = en(nc.sbuf_tensor("xres", [128, 2 * D], F32))
                xn2w = en(nc.sbuf_tensor("xn2w", [128, 2 * D], BF16))
                psW = en(nc.psum_tensor("psW", [128, 2 * D], F32))
                p = Plan(tot)
                for dmc in range(DMC):
                    p.dma("gpsimd", woT_sb[:, dmc * D:(dmc + 1) * D],
                          woT_in[dmc * 128:(dmc + 1) * 128, :],
                          waits=[at("ag_w"), at("atto_done")] if dmc == 0
                          else [])
                mark("wo_loaded", "qG")

                wo_ev, ln2_ap, xn2_ap, x2w, xn2w_dma, xadd = \
                    {}, {}, {}, {}, {}, {}
                for sc in range(SC):
                    par = sc % 2
                    xw = []
                    if sc >= 2:
                        xw.append(("cS", xn2_ap[sc - 2]))
                        xw.append(("qG", x2w[sc - 2]))
                    xld = p.dma("sync", xres[:, par * D:par * D + D],
                                x_in[sc * 128:(sc + 1) * 128, :], waits=xw)
                    w = [at("wo_loaded"), at("atto_done")]
                    if sc >= 2:
                        w.append(("cV", wo_ev[sc - 2]))

                    def mmW(e, sc=sc, par=par):
                        for hh in range(H):
                            for nq in range(4):
                                ins = e.matmul(
                                    psW[:, par * D + nq * 512:
                                        par * D + nq * 512 + 512],
                                    attoT[:, hh * R + sc * 128:
                                          hh * R + sc * 128 + 128],
                                    woT_sb[:, hh * D + nq * 512:
                                           hh * D + nq * 512 + 512],
                                    start=(hh == 0), stop=(hh == H - 1))
                        return ins
                    v_mm = p.add("tensor", mmW, waits=w, inc="cT")
                    evw = [("cT", v_mm)]
                    if sc >= 2:
                        evw.append(("cS", ln2_ap[sc - 2]))
                    wo_ev[sc] = p.add("vector",
                                      lambda e, par=par: e.tensor_copy(
                                          aproj[:, par * D:par * D + D],
                                          psW[:, par * D:par * D + D]),
                                      waits=evw, inc="cV")
                    apw = [("cV", xadd[sc - 1])] if sc >= 1 else []
                    ln2_ap[sc] = plan_ln(p, sc, aproj[:, par * D:par * D + D],
                                         lnout[:], [("cV", wo_ev[sc])],
                                         apply_waits=apw)
                    v_x2 = p.add("vector",
                                 lambda e, par=par: e.tensor_tensor(
                                     xres[:, par * D:par * D + D],
                                     xres[:, par * D:par * D + D],
                                     lnout[:], ALU.add),
                                 waits=[("cS", ln2_ap[sc]), ("qS", xld)],
                                 inc="cV")
                    xadd[sc] = v_x2
                    x2w[sc] = p.dma("gpsimd",
                                    x2_hbm[sc * 128:(sc + 1) * 128, :],
                                    xres[:, par * D:par * D + D],
                                    waits=[("cV", v_x2)])
                    xn2_ap[sc] = plan_ln(p, sc, xres[:, par * D:par * D + D],
                                         xn2w[:, par * D:par * D + D],
                                         [("cV", v_x2)])
                    xn2w_dma[sc] = p.dma("gpsimd",
                                         xn2_hbm[sc * 128:(sc + 1) * 128, :],
                                         xn2w[:, par * D:par * D + D],
                                         waits=[("cS", xn2_ap[sc])])
                mark("x2_written", "qG")
                mark("xn2_written", "qG")
                for dmc in range(DMC):
                    p.dma("sync", xnT[:, dmc * R:(dmc + 1) * R],
                          xn2_hbm[:, dmc * 128:(dmc + 1) * 128],
                          waits=[at("xn2_written")], transpose=True)
                mark("xn2T_ready", "qS")
                if lvl == 3:
                    p.add("gpsimd", lambda e: e.nop(),
                          waits=[("qS", tot.get("qS", 0)), ("qG", tot.get("qG", 0)),
                                 ("cV", tot.get("cV", 0)), ("cS", tot.get("cS", 0)),
                                 ("cT", tot.get("cT", 0)), ("cAG", tot.get("cAG", 0))])
                    p.emit(nc, sems)
                    dump_and_finish(src_ap_hbm_bf16=xn2_hbm[:])
                    return nc
                p.emit(nc, sems)

        with ExitStack() as _stD:
            en = _stD.enter_context
            hT = en(nc.sbuf_tensor("hT", [128, ICH * R], BF16))
            wstage = en(nc.sbuf_tensor("wstage", [128, 4 * DMC * 128], BF16))
            sgbuf = en(nc.sbuf_tensor("sgbuf", [128, R], BF16))
            dnstage = en(nc.sbuf_tensor("dnstage", [128, 2 * ICH * 128], BF16))
            ytT = en(nc.sbuf_tensor("ytT", [128, 2 * R], BF16))
            ywin = en(nc.sbuf_tensor("ywin", [128, D], BF16))
            x2win = en(nc.sbuf_tensor("x2win", [128, D], F32))
            p = Plan(tot)

            with nc.psum_tensor("psG", [128, 4 * R], F32) as psG:
                gu_mm, silu_done, hdone, wld = {}, {}, {}, {}
                for ic in range(ICH):
                    par = ic % 2
                    w = [at("ag_w")]
                    if ic == 0:
                        w.append(at("xn2T_ready"))
                    if ic >= 2:
                        w.append(("cT", gu_mm[ic - 2]))
                    for half, oc in ((0, ic), (1, ICH + ic)):
                        rank, oi = divmod(oc, OCH)
                        off = rank * WSEC_E + oi * (128 * DMC * 128)
                        q = (2 * par + half) * (DMC * 128)
                        wld[(ic, half)] = p.dma(
                            "gpsimd" if half == 1 else "sync",
                            wstage[:, q:q + DMC * 128],
                            w_full[off:off + 128 * DMC * 128].rearrange(
                                "(q f) -> q f", q=128),
                            waits=list(w) if half == 0 or ic < 2 else
                            [("cT", gu_mm[ic - 2])])

                    wmm = [("qS", wld[(ic, 0)]), ("qG", wld[(ic, 1)]),
                           at("xn2T_ready")]
                    if ic >= 2:
                        wmm.append(("cS", silu_done[ic - 2]))
                        wmm.append(("cV", hdone[ic - 2]))

                    def mmGU(e, ic=ic, par=par):
                        for half in range(2):
                            q = (2 * par + half) * (DMC * 128)
                            pbase = (2 * par + half) * R
                            for dmc in range(DMC):
                                for sb in range(2):
                                    ins = e.matmul(
                                        psG[:, pbase + sb * 512:
                                            pbase + sb * 512 + 512],
                                        wstage[:, q + dmc * 128:
                                               q + dmc * 128 + 128],
                                        xnT[:, dmc * R + sb * 512:
                                            dmc * R + sb * 512 + 512],
                                        start=(dmc == 0),
                                        stop=(dmc == DMC - 1))
                        return ins
                    gu_mm[ic] = p.add("tensor", mmGU, waits=wmm, inc="cT")
                    silu_done[ic] = p.add(
                        "scalar",
                        lambda e, par=par: e.activation(
                            out=sgbuf[:],
                            in_=psG[:, 2 * par * R:2 * par * R + R],
                            func=AF.Silu),
                        waits=[("cT", gu_mm[ic])], inc="cS")
                    hdone[ic] = p.add(
                        "vector",
                        lambda e, ic=ic, par=par: e.tensor_tensor(
                            hT[:, ic * R:(ic + 1) * R], sgbuf[:],
                            psG[:, (2 * par + 1) * R:(2 * par + 1) * R + R],
                            ALU.mult),
                        waits=[("cS", silu_done[ic])], inc="cV")
                mark("hT_done", "cV")
                if upto == "D1h":
                    p.add("gpsimd", lambda e: e.nop(),
                          waits=[("qS", tot.get("qS", 0)), ("qG", tot.get("qG", 0)),
                                 ("cV", tot.get("cV", 0)), ("cS", tot.get("cS", 0)),
                                 ("cT", tot.get("cT", 0)), ("cAG", tot.get("cAG", 0))])
                    p.emit(nc, sems)
                    dump_and_finish(src_sbuf=hT[:, 0:16 * R])
                    return nc

            with nc.psum_tensor("psY", [128, 2 * R], F32) as psY:
                dn_mm, yt_ev, ytw = {}, {}, {}
                for dmc in range(DMC):
                    par = dmc % 2
                    rank, dmci = divmod(dmc, 2)
                    off = rank * WSEC_E + GUPART_E + dmci * (ICH * 128 * 128)
                    w = [at("hT_done")]
                    if dmc >= 2:
                        w.append(("cT", dn_mm[dmc - 2]))
                    dnld = p.dma(
                        "sync",
                        dnstage[:, par * (ICH * 128):
                                (par + 1) * (ICH * 128)].rearrange(
                            "q (ic f) -> q ic f", ic=ICH),
                        w_full[off:off + ICH * 128 * 128].rearrange(
                            "(ic q f) -> q ic f", ic=ICH, q=128),
                        waits=w)
                    wmm = [("qS", dnld), at("hT_done")]
                    if dmc >= 2:
                        wmm.append(("cV", yt_ev[dmc - 2]))

                    def mmDN(e, par=par):
                        for ic in range(ICH):
                            for sb in range(2):
                                ins = e.matmul(
                                    psY[:, par * R + sb * 512:
                                        par * R + sb * 512 + 512],
                                    dnstage[:, par * (ICH * 128) + ic * 128:
                                            par * (ICH * 128) + ic * 128 + 128],
                                    hT[:, ic * R + sb * 512:
                                       ic * R + sb * 512 + 512],
                                    start=(ic == 0), stop=(ic == ICH - 1))
                        return ins
                    dn_mm[dmc] = p.add("tensor", mmDN, waits=wmm, inc="cT")
                    yt_ev[dmc] = p.add("vector",
                                       lambda e, par=par: e.tensor_copy(
                                           ytT[:, par * R:par * R + R],
                                           psY[:, par * R:par * R + R]),
                                       waits=[("cT", dn_mm[dmc])], inc="cV")
                    ytw[dmc] = p.dma("gpsimd",
                                     yT_hbm[dmc * 128:(dmc + 1) * 128, :],
                                     ytT[:, par * R:par * R + R],
                                     waits=[("cV", yt_ev[dmc])])
                mark("yT_written", "qG")
                if upto == "D2y":
                    p.add("gpsimd", lambda e: e.nop(),
                          waits=[("qS", tot.get("qS", 0)), ("qG", tot.get("qG", 0)),
                                 ("cV", tot.get("cV", 0)), ("cS", tot.get("cS", 0)),
                                 ("cT", tot.get("cT", 0)), ("cAG", tot.get("cAG", 0))])
                    p.emit(nc, sems)
                    dump_and_finish(src_ap_hbm_bf16=yT_hbm[:])
                    return nc

                ln3_ap, outv, outdma = {}, {}, {}
                for sc in range(SC):
                    w = [at("yT_written")]
                    if sc >= 1:
                        w.append(("cS", ln3_ap[sc - 1]))
                    yld = p.dma("sync", ywin[:],
                                yT_hbm[:, sc * 128:(sc + 1) * 128],
                                waits=w, transpose=True)
                    x2ld = p.dma("sync", x2win[:],
                                 x2_hbm[sc * 128:(sc + 1) * 128, :],
                                 waits=([("cV", outv[sc - 1])] if sc >= 1
                                        else []) + [at("x2_written")])
                    v_s1 = p.add("scalar",
                                 lambda e: e.activation(
                                     out=junk[:], in_=ywin[:], func=AF.Square,
                                     accum_out=stats[:, 66:67]),
                                 waits=[("qS", yld)], inc="cS")
                    v_s2 = p.add("scalar",
                                 lambda e: e.activation(
                                     out=junk[:], in_=ywin[:], func=AF.Copy,
                                     accum_out=stats[:, 67:68]),
                                 waits=[("cS", v_s1)], inc="cS")

                    def smalls(e):
                        mu = stats[:, 68:69]
                        t = stats[:, 69:70]
                        scl = stats[:, 70:71]
                        tmp = stats[:, 71:72]
                        u1 = stats[:, 75:76]
                        e.tensor_scalar_mul(mu, stats[:, 67:68], 1.0 / D)
                        e.tensor_scalar_mul(t, stats[:, 66:67], 1.0 / D)
                        e.tensor_tensor(tmp, mu, mu, ALU.mult)
                        e.tensor_scalar(scl, t, 1e-5, 1e-11, ALU.mult, ALU.add)
                        e.tensor_tensor(u1, t, tmp, ALU.subtract)
                        e.engine_nop()
                        return e.tensor_tensor(stats[:, 71:72], u1, scl,
                                               ALU.add)
                    v_sm = p.add("vector", smalls, waits=[("cS", v_s2)],
                                 inc="cV")
                    v_sq = p.add("scalar",
                                 lambda e: e.activation(out=stats[:, 72:73],
                                                        in_=stats[:, 71:72],
                                                        func=AF.Sqrt),
                                 waits=[("cV", v_sm)], inc="cS")

                    def smalls2(e):
                        rstd = stats[:, 73:74]
                        nmr = stats[:, 74:75]
                        e.reciprocal(rstd, stats[:, 72:73])
                        e.engine_nop()
                        return e.scalar_tensor_tensor(nmr, stats[:, 68:69],
                                                      -1.0, rstd, ALU.mult,
                                                      ALU.mult)
                    v_sm2 = p.add("vector", smalls2, waits=[("cS", v_sq)],
                                  inc="cV")
                    lw = [("cV", v_sm2)]
                    if sc >= 1:
                        lw.append(("cV", outv[sc - 1]))
                    ln3_ap[sc] = p.add("scalar",
                                       lambda e: e.activation(
                                           out=lnout[:], in_=ywin[:],
                                           func=AF.Identity,
                                           scale=stats[:, 73:74],
                                           bias=stats[:, 74:75]),
                                       waits=lw, inc="cS")
                    ow = [("cS", ln3_ap[sc]), ("qS", x2ld)]
                    if sc >= 1:
                        ow.append(("qS", outdma[sc - 1]))
                    outv[sc] = p.add("vector",
                                     lambda e: e.tensor_tensor(
                                         x2win[:], x2win[:], lnout[:],
                                         ALU.add),
                                     waits=ow, inc="cV")
                    outdma[sc] = p.dma("sync",
                                       y_out[sc * 128:(sc + 1) * 128, :],
                                       x2win[:], waits=[("cV", outv[sc])])
                p.add("sync", lambda e: e.nop(), waits=[("qS", tot["qS"])])
                p.emit(nc, sems)

    return nc


_NC_CACHE = None


def kernel(**inputs):
    global _NC_CACHE
    g = {k: np.asarray(v) for k, v in inputs.items()}
    x = np.ascontiguousarray(
        g["L0_attn_x"].reshape(8192, D).astype(np.float32, copy=False))
    wqT = np.ascontiguousarray(g["L0_attn_W_q"].T.astype(np.float32, copy=False))
    wkvT = np.ascontiguousarray(
        g["L0_attn_W_down_kv"].T.astype(np.float32, copy=False))
    wupkT = np.ascontiguousarray(
        np.transpose(g["L0_attn_W_up_k"], (2, 0, 1)).reshape(L, D).astype(
            np.float32, copy=False))
    wupvT = np.ascontiguousarray(
        np.transpose(g["L0_attn_W_up_v"], (2, 0, 1)).reshape(L, D).astype(
            np.float32, copy=False))
    woT = np.ascontiguousarray(g["L0_attn_W_o"].T.astype(np.float32, copy=False))
    gu_u8 = np.ascontiguousarray(g["L0_ffn_gate_up_packed"]).view(
        np.uint8).reshape(2 * I_FF, D // 2)
    gu_scT = np.ascontiguousarray(g["L0_ffn_gate_up_scales"].T)
    dn_u8 = np.ascontiguousarray(g["L0_ffn_down_packed"]).view(
        np.uint8).reshape(D, I_FF // 2)
    dn_scT = np.ascontiguousarray(g["L0_ffn_down_scales"].T)

    if UPTO != "full":
        nc = build_nc(UPTO)
    else:
        if _NC_CACHE is None:
            _NC_CACHE = build_nc()
        nc = _NC_CACHE

    in_maps = []
    for c in range(NCORE):
        in_maps.append({
            "x": np.ascontiguousarray(x[c * R:(c + 1) * R]),
            "wqT": wqT, "wkvT": wkvT, "wupkT": wupkT, "wupvT": wupvT,
            "woT": woT,
            "gu_u8": np.ascontiguousarray(gu_u8[c * OSH:(c + 1) * OSH]),
            "gu_sc": np.ascontiguousarray(gu_scT[c * OSH:(c + 1) * OSH]),
            "dn_u8": np.ascontiguousarray(dn_u8[c * DSH:(c + 1) * DSH]),
            "dn_sc": np.ascontiguousarray(dn_scT[c * DSH:(c + 1) * DSH]),
        })
    res = run_bass_kernel_spmd(nc, in_maps, core_ids=list(range(NCORE)))
    out = np.concatenate([res.results[c]["y"] for c in range(NCORE)], axis=0)
    return out.reshape(1, 8192, D).astype(np.float32, copy=False)
